# version 07
# baseline (speedup 1.0000x reference)
import numpy as np
import concourse.bass as bass
import concourse.mybir as mybir
from concourse.bass_utils import run_bass_kernel_spmd
from concourse import library_config
from contextlib import ExitStack

F32 = mybir.dt.float32
I32 = mybir.dt.int32
I16 = mybir.dt.int16
AF = mybir.ActivationFunctionType
ALU = mybir.AluOpType
AX = mybir.AxisListType
ds = bass.ds

NP = 128
D = 1024
T = 4096
TOK = 2048
HD = 64
NPAIR = 4
DI = 4096
NE = 8
CLOC = 768
CAP = 5120
FBB = NE * CLOC
GD1 = NE * CLOC
GD2 = NE * CLOC + TOK
GROWS = NE * CLOC + 2 * TOK
YROWS = NE * CLOC + TOK
LNEPS = 1e-5
RINV = 1.0 / D


def _free_inst_regs(nc, bi, engine):
    import bass_rust as _br
    inst = bi.ins
    for a in list(inst.ins):
        if isinstance(a, mybir.RegisterAccess):
            try:
                nc.free_register(_br.RegisterHandle(name=a.regref, engine=engine))
            except Exception:
                pass
    return bi


def _mk_runtime_ap(base_ap, static_off, dyn_expr, pattern):
    total = 1
    for st, ct in pattern:
        total = max(total, st * ct)
    sl = base_ap[ds(dyn_expr + static_off, 1)]
    return bass.AP(sl.tensor, sl.offset, pattern)


class S:
    def __init__(self, nc, name):
        self.h = nc.alloc_semaphore(name=name)
        self.n = 0

    def tick(self, instr, k=1):
        instr.then_inc(self.h, k)
        self.n += k
        return instr


class KB:
    def __init__(self, taps=(), p4mode="full"):
        self.p4mode = p4mode
        nc = bass.Bass("TRN2", target_bir_lowering=False)
        self.nc = nc
        self.taps = taps
        self.stack = ExitStack()
        inp = {}
        def I(name, shape):
            inp[name] = nc.dram_tensor(name, shape, F32, kind="ExternalInput").ap()
        I("x", [T, D]); I("xo", [TOK, D])
        I("wq", [512, D]); I("wk", [512, D]); I("wv", [512, D])
        I("bq", [512]); I("bk", [512]); I("bv", [512])
        I("wo", [D, 512]); I("bo", [D])
        I("rb", [31, 8])
        I("wr", [NE, D]); I("w1", [NE, D, DI]); I("b1", [NE, DI])
        I("w2", [NE, DI, D]); I("b2", [NE, D])
        I("fw1", [D, DI]); I("fb1", [DI]); I("fw2", [DI, D]); I("fb2", [D])
        I("cmod", [1, D]); I("wmod", [D, 6 * D])
        inp["cslot"] = nc.dram_tensor("cslot", [1, 1], I32, kind="ExternalInput").ap()
        self.inp = inp
        self.out_y = nc.dram_tensor("y", [TOK, D], F32, kind="ExternalOutput").ap()
        dr = {}
        def DT(name, shape, dt=F32, **kw):
            dr[name] = nc.dram_tensor(name, shape, dt, **kw).ap()
        DT("wto", [512, D])
        DT("qfm", [512, T]); DT("kfm", [512, T]); DT("vtm", [T, 512])
        DT("afm", [512, T])
        DT("rsin", [T, D]); DT("rsout", [TOK, D])
        DT("x2d", [TOK, D]); DT("h2tm", [TOK, D]); DT("h2fm", [D, TOK])
        DT("lgin", [TOK, NE])
        DT("lgout", [8 * TOK, NE], addr_space="Shared")
        DT("gbuf", [GROWS, D]); DT("ybuf", [YROWS, D])
        DT("rline", [8, 384])
        DT("idxb", [4 * 16384], I32)
        DT("idxwd", [4 * 2048], I32)
        DT("wfb", [2 * 16384])
        self.dr = dr
        self.ld = S(nc, "ld")
        self.gd = S(nc, "gd")
        self.pe = S(nc, "pe")
        self.act = S(nc, "act")
        self.dve = S(nc, "dve")
        self.pool = S(nc, "pool")
        self.cc = S(nc, "cc")

    def sb(self, name, shape, dt=F32, stk=None):
        return (stk or self.stack).enter_context(self.nc.sbuf_tensor(name, shape, dt))

    def ps(self, name, shape, dt=F32, stk=None):
        return (stk or self.stack).enter_context(self.nc.psum_tensor(name, shape, dt))

    def sec(self, eng, fn):
        getattr(self.block, eng)(fn)

    def join_all(self):
        sems = (self.ld, self.gd, self.pe, self.act, self.dve, self.pool, self.cc)
        tgts = [(s.h, s.n) for s in sems]
        for eng in ("sync", "tensor", "scalar", "vector", "gpsimd"):
            def jw(e, tgts=tgts):
                for h, n in tgts:
                    if n:
                        e.wait_ge(h, n)
            self.sec(eng, jw)

    def build(self):
        nc = self.nc
        with nc.Block() as block:
            self.block = block
            self.phase0()
            self.phase1()
            self.phase2()
            self.phase3()
            self.phase4()
            self.emit_taps()
        self.stack.close()
        mybir.codegen_inst_isa_subclasses(nc)
        return nc

    def phase0(self):
        nc, ld, gd, pe, act, dve, pool = (
            self.nc, self.ld, self.gd, self.pe, self.act, self.dve, self.pool)

        self.idn = self.sb("idn", [NP, NP])
        self.ones_col = self.sb("ones_col", [NP, 1])
        self.ones_row = self.sb("ones_row", [1, 512])
        self.Ltri = self.sb("Ltri", [NP, NP])
        self.Lstr = self.sb("Lstr", [NP, NP])
        self.rev8 = self.sb("rev8", [NP, 1024])
        self.tloc = self.sb("tloc", [NP, 128])
        self.rep8 = self.sb("rep8", [8, NP])
        self.mod = {"gb": self.sb("mod_gb", [NP, D])}
        self.stkA = ExitStack()
        for k in ("ca1", "sa", "cb1", "sb", "ga"):
            self.mod[k] = self.sb("mod_" + k, [NP, D], stk=self.stkA)
        self.bob = self.sb("bob", [NP, D], stk=self.stkA)
        self.bvb = self.sb("bvb", [NP, 512], stk=self.stkA)
        self.bq_sb = self.sb("bq_sb", [NP, 4], stk=self.stkA)
        self.bk_sb = self.sb("bk_sb", [NP, 4], stk=self.stkA)
        self.rb_sb = self.sb("rb_sb", [31, 8], stk=self.stkA)
        self.c0row = self.sb("c0row", [1, 8], stk=self.stkA)
        self.c30row = self.sb("c30row", [1, 8], stk=self.stkA)
        self.bmaxp = self.sb("bmaxp", [1, 4], stk=self.stkA)
        self.pat = self.sb("pat", [NP, 8 * 256], stk=self.stkA)
        self.wrt = self.sb("wrt", [NP, 8 * 8], stk=self.stkA)
        self.line_sb = self.sb("line_sb", [8, 384], stk=self.stkA)

        self.stk_w = ExitStack()
        self.wqt = [self.sb(f"wqt{i}", [NP, 512], stk=self.stk_w) for i in range(8)]
        self.wkt = [self.sb(f"wkt{i}", [NP, 512], stk=self.stk_w) for i in range(8)]
        self.wvt = [self.sb(f"wvt{i}", [NP, 512], stk=self.stk_w) for i in range(8)]

        stk0 = ExitStack()
        scs = self.sb("scs", [NP, 8], stk=stk0)
        modrow = self.sb("modrow", [1, 6 * D], stk=stk0)
        wmt = [self.sb(f"wmt{i}", [NP, 8 * 512], stk=stk0) for i in range(2)]
        wraw = self.sb("wraw", [NP, 4 * D], stk=stk0)
        wstg = self.sb("wstg", [NP, 512], stk=stk0)
        rbt = self.sb("rbt", [8, 31], stk=stk0)
        borow = self.sb("borow", [1, D], stk=stk0)
        bvrow = self.sb("bvrow", [1, 512], stk=stk0)
        gtmp1 = self.sb("gtmp1", [1, D], stk=stk0)
        gtmp2 = self.sb("gtmp2", [1, D], stk=stk0)
        bmx = self.sb("bmx", [8, 1], stk=stk0)
        bmrow = self.sb("bmrow", [1, 8], stk=stk0)
        ps_mod = self.ps("ps_mod", [1, 512], stk=stk0)
        ps_tp = self.ps("ps_tp", [NP, 512], stk=stk0)
        ps_tp2 = self.ps("ps_tp2", [NP, 512], stk=stk0)

        def g0(g):
            pool.tick(g.memset(self.idn[:], 0.0))
            pool.tick(g.memset(self.Ltri[:], 0.0))
            pool.tick(g.memset(self.Lstr[:], 0.0))
            g.wait_ge(pool.h, pool.n)
            pool.tick(g.affine_select(self.idn[:], self.idn[:], pattern=[[-1, NP]],
                      compare_op=ALU.not_equal, fill=1.0, base=0, channel_multiplier=1))
            pool.tick(g.affine_select(self.Ltri[:], self.Ltri[:], pattern=[[-1, NP]],
                      compare_op=ALU.is_gt, fill=1.0, base=0, channel_multiplier=1))
            pool.tick(g.affine_select(self.Lstr[:], self.Lstr[:], pattern=[[-1, NP]],
                      compare_op=ALU.is_ge, fill=1.0, base=0, channel_multiplier=1))
            pool.tick(g.iota(self.rev8[:], pattern=[[0, NP], [-1, 8]], base=8,
                             channel_multiplier=0,
                             allow_small_or_imprecise_dtypes=True))
            pool.tick(g.iota(self.tloc[:], pattern=[[0, 8], [NP, 16]], base=0,
                             channel_multiplier=1,
                             allow_small_or_imprecise_dtypes=True))
            pool.tick(g.memset(self.rep8[:], 1.0))
            g.wait_ge(pool.h, pool.n)
            pool.tick(g.affine_select(self.rep8[:], self.rep8[:], pattern=[[1, NP]],
                      compare_op=ALU.is_ge, fill=0.0, base=0,
                      channel_multiplier=-16))
            g.wait_ge(pool.h, pool.n)
            pool.tick(g.affine_select(self.rep8[:], self.rep8[:], pattern=[[-1, NP]],
                      compare_op=ALU.is_ge, fill=0.0, base=15,
                      channel_multiplier=16))
            g.load_library(library_config.mlp)
            pool.tick(g.engine_nop())
        self.sec("gpsimd", g0)
        self.pool_const = pool.n

        def v0(v):
            dve.tick(v.memset(self.ones_col[:], 1.0))
            dve.tick(v.memset(self.ones_row[:], 1.0))
        self.sec("vector", v0)

        def s0(sync):
            ld.tick(sync.dma_start(out=scs[:], in_=bass.AP(
                self.inp["cmod"].tensor, 0, [[1, NP], [NP, 8]]),
                allow_slow_non_contiguous=True), 16)
            ld.tick(sync.dma_start(out=self.rb_sb[:], in_=self.inp["rb"][:, :]), 16)
            ld.tick(sync.dma_start(out=self.bq_sb[:], in_=bass.AP(
                self.inp["bq"].tensor, 0, [[1, NP], [NP, 4]]),
                allow_slow_non_contiguous=True), 16)
            ld.tick(sync.dma_start(out=self.bk_sb[:], in_=bass.AP(
                self.inp["bk"].tensor, 0, [[1, NP], [NP, 4]]),
                allow_slow_non_contiguous=True), 16)
            ld.tick(sync.dma_start(out=borow[0:1, :],
                    in_=self.inp["bo"][:].rearrange("(a d) -> a d", a=1)), 16)
            ld.tick(sync.dma_start(out=bvrow[0:1, :],
                    in_=self.inp["bv"][:].rearrange("(a d) -> a d", a=1)), 16)
            ld.tick(sync.dma_start(out=wmt[0][:], in_=bass.AP(
                self.inp["wmod"].tensor, 0, [[6 * D, NP], [NP * 6 * D, 8], [1, 512]])), 16)
        self.sec("sync", s0)
        ld_small = ld.n

        def a0(s):
            s.wait_ge(ld.h, ld_small)
            act.tick(s.activation(scs[:], scs[:], AF.Silu))
        self.sec("scalar", a0)
        act_silu = act.n

        pe_after_mm = {}
        ld_wmt = {0: ld_small}
        for n in range(12):
            buf = wmt[n % 2]
            if n + 1 < 12:
                def sload(sync, n=n):
                    if n >= 1:
                        sync.wait_ge(pe.h, pe_after_mm[n - 1])
                    ld.tick(sync.dma_start(out=wmt[(n + 1) % 2][:], in_=bass.AP(
                        self.inp["wmod"].tensor, (n + 1) * 512,
                        [[6 * D, NP], [NP * 6 * D, 8], [1, 512]])), 16)
                    ld_wmt[n + 1] = ld.n
                self.sec("sync", sload)
            def t0(t, n=n, buf=buf):
                t.wait_ge(ld.h, ld_wmt[n])
                if n == 0:
                    t.wait_ge(act.h, act_silu)
                else:
                    t.wait_ge(act.h, act.n)
                for dc in range(8):
                    pe.tick(t.matmul(ps_mod[:], scs[:, dc:dc + 1],
                                     buf[:, dc * 512:(dc + 1) * 512],
                                     start=(dc == 0), stop=(dc == 7)))
                pe_after_mm[n] = pe.n
            self.sec("tensor", t0)
            def a1(s, n=n):
                s.wait_ge(pe.h, pe_after_mm[n])
                act.tick(s.activation(modrow[0:1, n * 512:(n + 1) * 512],
                                      ps_mod[:], AF.Copy))
            self.sec("scalar", a1)
        act_mod = act.n

        mr = modrow
        sa_r, ca_r, ga_r = mr[0:1, 0:D], mr[0:1, D:2 * D], mr[0:1, 2 * D:3 * D]
        sb_r, cb_r, gb_r = mr[0:1, 3 * D:4 * D], mr[0:1, 4 * D:5 * D], mr[0:1, 5 * D:6 * D]

        def v1(v):
            v.wait_ge(act.h, act_mod)
            dve.tick(v.tensor_scalar_add(ca_r, ca_r, 1.0))
            dve.tick(v.tensor_scalar_add(cb_r, cb_r, 1.0))
            for g_r in (ga_r, gb_r):
                dve.tick(v.tensor_scalar(gtmp1[0:1, :], g_r, 0.0, None, ALU.is_ge))
                v.wait_ge(dve.h, dve.n)
                dve.tick(v.tensor_scalar(gtmp1[0:1, :], gtmp1[0:1, :], 2.0, 1.0,
                                         ALU.mult, ALU.subtract))
                dve.tick(v.tensor_scalar_mul(gtmp2[0:1, :], g_r, -1.0))
                v.wait_ge(dve.h, dve.n)
                dve.tick(v.tensor_max(gtmp2[0:1, :], gtmp2[0:1, :], g_r))
                v.wait_ge(dve.h, dve.n)
                dve.tick(v.tensor_scalar_max(gtmp2[0:1, :], gtmp2[0:1, :], 0.25))
                v.wait_ge(dve.h, dve.n)
                dve.tick(v.tensor_mul(g_r, gtmp1[0:1, :], gtmp2[0:1, :]))
                v.wait_ge(dve.h, dve.n)
        self.sec("vector", v1)
        dve_mod = dve.n

        bjobs = [(sa_r, self.mod["sa"], 1024), (ca_r, self.mod["ca1"], 1024),
                 (ga_r, self.mod["ga"], 1024), (sb_r, self.mod["sb"], 1024),
                 (cb_r, self.mod["cb1"], 1024), (gb_r, self.mod["gb"], 1024),
                 (borow[0:1, :], self.bob, 1024), (bvrow[0:1, :], self.bvb, 512)]
        for (rowap, dstt, w) in bjobs:
            for sgb in range(w // 512):
                def tbc(t, rowap=rowap, sgb=sgb):
                    t.wait_ge(dve.h, dve_mod)
                    t.wait_ge(ld.h, ld_small)
                    t.wait_ge(act.h, act.n)
                    pe.tick(t.matmul(ps_tp2[:], self.ones_row[0:1, 0:128],
                                     rowap[0:1, sgb * 512:(sgb + 1) * 512],
                                     start=True, stop=True))
                self.sec("tensor", tbc)
                pe_bc = pe.n
                def abc(s, dstt=dstt, sgb=sgb, pe_bc=pe_bc):
                    s.wait_ge(pe.h, pe_bc)
                    act.tick(s.activation(dstt[:, sgb * 512:(sgb + 1) * 512],
                                          ps_tp2[:], AF.Copy))
                self.sec("scalar", abc)
        self.pool_mod = pool.n

        def t1(t):
            t.wait_ge(ld.h, ld_small)
            t.wait_ge(pool.h, self.pool_const)
            pe.tick(t.transpose(ps_tp2[0:8, 0:31], self.rb_sb[:], self.idn[0:31, 0:31]))
        self.sec("tensor", t1)
        pe_rbt = pe.n

        def v2(v):
            v.wait_ge(pe.h, pe_rbt)
            dve.tick(v.tensor_copy(rbt[:], ps_tp2[0:8, 0:31]))
            dve.tick(v.memset(self.line_sb[:], 0.0))
            v.wait_ge(dve.h, dve.n)
            dve.tick(v.tensor_scalar(self.line_sb[:, 0:177], self.line_sb[:, 0:177],
                                     rbt[:, 30:31], None, ALU.add))
            dve.tick(v.tensor_scalar(self.line_sb[:, 207:384], self.line_sb[:, 207:384],
                                     rbt[:, 0:1], None, ALU.add))
            for n in range(177, 207):
                dve.tick(v.tensor_copy(self.line_sb[:, n:n + 1], rbt[:, 206 - n:207 - n]))
            dve.tick(v.tensor_reduce(bmx[:], rbt[:], AX.X, ALU.max))
        self.sec("vector", v2)
        dve_line = dve.n

        def t3(t):
            t.wait_ge(dve.h, dve_line)
            pe.tick(t.transpose(ps_tp2[0:1, 32:40], bmx[:], self.idn[0:8, 0:8]))
        self.sec("tensor", t3)
        pe_bmx = pe.n

        def v4(v):
            v.wait_ge(pe.h, pe_bmx)
            dve.tick(v.tensor_copy(bmrow[:], ps_tp2[0:1, 32:40]))
            v.wait_ge(dve.h, dve.n)
            dve.tick(v.tensor_max(self.bmaxp[:],
                                  bass.AP(bmrow, 0, [[8, 1], [2, 4]]),
                                  bass.AP(bmrow, 1, [[8, 1], [2, 4]])))
        self.sec("vector", v4)

        def g2(g):
            g.wait_ge(ld.h, ld_small)
            gd.tick(g.dma_start(out=self.c0row[:], in_=self.rb_sb[0:1, :]), 16)
            gd.tick(g.dma_start(out=self.c30row[:], in_=self.rb_sb[30:31, :]), 16)
            g.wait_ge(dve.h, dve_line)
            gd.tick(g.dma_start(out=self.dr["rline"][:, :], in_=self.line_sb[:]), 16)
            g.wait_ge(gd.h, gd.n)
            for p in range(NP):
                gd.tick(g.dma_start(
                    out=bass.AP(self.pat, p * 2048, [[2048, 1], [1, 2048]]),
                    in_=self.line_sb[:, 127 - p: 383 - p]), 16)
        self.sec("gpsimd", g2)
        self.gd_pat = gd.n

        for wname, dstl in (("wq", self.wqt), ("wk", self.wkt), ("wv", self.wvt)):
            def sW(sync, wname=wname):
                sync.wait_ge(pe.h, pe.n)
                ld.tick(sync.dma_start(out=wraw[:], in_=bass.AP(
                    self.inp[wname].tensor, 0, [[D, NP], [NP * D, 4], [1, D]])), 16)
            self.sec("sync", sW)
            ld_w = ld.n
            for dc in range(8):
                def tW(t, dc=dc, ld_w=ld_w):
                    if dc == 0:
                        t.wait_ge(ld.h, ld_w)
                        t.wait_ge(pool.h, self.pool_const)
                    t.wait_ge(act.h, act.n)
                    for ic in range(4):
                        pe.tick(t.transpose(ps_tp[:, ic * 128:(ic + 1) * 128],
                                            wraw[:, ic * D + dc * 128: ic * D + (dc + 1) * 128],
                                            self.idn[:]))
                self.sec("tensor", tW)
                def aW(s, dc=dc, dstl=dstl):
                    s.wait_ge(pe.h, pe.n)
                    act.tick(s.activation(dstl[dc][:], ps_tp[:], AF.Copy))
                self.sec("scalar", aW)
        def sWo(sync):
            sync.wait_ge(pe.h, pe.n)
            ld.tick(sync.dma_start(out=wraw[:], in_=bass.AP(
                self.inp["wo"].tensor, 0, [[512, NP], [NP * 512, 8], [1, 512]])), 16)
        self.sec("sync", sWo)
        ld_wo = ld.n
        for vc in range(4):
            for jc in range(8):
                def tWo(t, vc=vc, jc=jc):
                    if vc == 0 and jc == 0:
                        t.wait_ge(ld.h, ld_wo)
                    t.wait_ge(act.h, act.n)
                    pe.tick(t.transpose(ps_tp2[:, 0:128],
                                        wraw[:, jc * 512 + vc * 128: jc * 512 + (vc + 1) * 128],
                                        self.idn[:]))
                self.sec("tensor", tWo)
                def aWo(s):
                    s.wait_ge(pe.h, pe.n)
                    s.wait_ge(ld.h, ld.n)
                    act.tick(s.activation(wstg[:, 0:128], ps_tp2[:, 0:128], AF.Copy))
                self.sec("scalar", aWo)
                def sWo2(sync, vc=vc, jc=jc):
                    sync.wait_ge(act.h, act.n)
                    ld.tick(sync.dma_start(
                        out=self.dr["wto"][vc * 128:(vc + 1) * 128, jc * 128:(jc + 1) * 128],
                        in_=wstg[:, 0:128]), 16)
                self.sec("sync", sWo2)
        def sWr(sync):
            sync.wait_ge(pe.h, pe.n)
            ld.tick(sync.dma_start(out=wraw[0:8, 0:D], in_=self.inp["wr"][:, :]), 16)
        self.sec("sync", sWr)
        ld_wr = ld.n
        def tWr(t):
            t.wait_ge(ld.h, ld_wr)
            t.wait_ge(act.h, act.n)
            for dc in range(8):
                pe.tick(t.transpose(ps_tp[0:128, dc * 8:dc * 8 + 8],
                                    wraw[0:8, dc * 128:(dc + 1) * 128],
                                    self.idn[0:8, 0:8]))
        self.sec("tensor", tWr)
        def aWr(s):
            s.wait_ge(pe.h, pe.n)
            act.tick(s.activation(self.wrt[:], ps_tp[0:128, 0:64], AF.Copy))
        self.sec("scalar", aWr)

        self.join_all()
        stk0.close()

    def phase1(self):
        nc, ld, pe, act, dve = self.nc, self.ld, self.pe, self.act, self.dve

        stk = ExitStack()
        xblk = [self.sb(f"xblk{i}", [NP, 4 * D], stk=stk) for i in range(2)]
        h1f = [self.sb(f"h1f{i}", [NP, 8 * 512], stk=stk) for i in range(2)]
        qstg = self.sb("qstg", [NP, 512], stk=stk)
        kstg = self.sb("kstg", [NP, 512], stk=stk)
        vstg = self.sb("vstg", [NP, 512], stk=stk)
        sq = self.sb("sq", [NP, 4 * D], stk=stk)
        smv = self.sb("smv", [NP, 4], stk=stk)
        ssqv = self.sb("ssqv", [NP, 4], stk=stk)
        mv = self.sb("mv", [NP, 4], stk=stk)
        m2v = self.sb("m2v", [NP, 4], stk=stk)
        varv = self.sb("varv", [NP, 4], stk=stk)
        rstdv = self.sb("rstdv", [NP, 4], stk=stk)
        sdv = self.sb("sdv", [NP, 4], stk=stk)
        epsv = self.sb("epsv", [NP, 1], stk=stk)
        xnv = self.sb("xnv", [NP, D], stk=stk)
        pstp = [self.ps(f"p1tp{i}", [NP, 512], stk=stk) for i in range(2)]
        psqk = [self.ps(f"p1qk{i}", [NP, 512], stk=stk) for i in range(2)]

        ca1, sa = self.mod["ca1"], self.mod["sa"]
        blk_done = {}
        def vep(v):
            dve.tick(v.memset(epsv[:], LNEPS))
        self.sec("vector", vep)
        dve_eps = dve.n

        for tb in range(8):
            xb = xblk[tb % 2]
            hb = h1f[tb % 2]
            def sx(sync, tb=tb, xb=xb):
                if tb >= 2:
                    sync.wait_ge(pe.h, blk_done[tb - 2])
                ld.tick(sync.dma_start(out=xb[:], in_=bass.AP(
                    self.inp["x"].tensor, tb * 512 * D,
                    [[D, NP], [NP * D, 4], [1, D]])), 16)
            self.sec("sync", sx)
            ld_x = ld.n

            for jj in range(4):
                xt = xb[:, jj * D:(jj + 1) * D]
                def aln(s, jj=jj, xt=xt, ld_x=ld_x):
                    if jj == 0:
                        s.wait_ge(ld.h, ld_x)
                    act.tick(s.activation(sq[:, jj * D:(jj + 1) * D], xt, AF.Square,
                                          accum_out=ssqv[:, jj:jj + 1]))
                self.sec("scalar", aln)
                act_sq = act.n
                def vln(v, jj=jj, xt=xt, ld_x=ld_x, act_sq=act_sq):
                    if jj == 0:
                        v.wait_ge(ld.h, ld_x)
                    dve.tick(v.reduce_sum(smv[:, jj:jj + 1], xt, AX.X))
                    v.wait_ge(dve.h, dve.n)
                    dve.tick(v.tensor_scalar_mul(mv[:, jj:jj + 1], smv[:, jj:jj + 1], RINV))
                    v.wait_ge(dve.h, dve.n)
                    dve.tick(v.scalar_tensor_tensor(
                        m2v[:, jj:jj + 1], mv[:, jj:jj + 1], 1.0, mv[:, jj:jj + 1],
                        ALU.mult, ALU.mult))
                    v.wait_ge(act.h, act_sq)
                    v.wait_ge(dve.h, dve.n)
                    dve.tick(v.scalar_tensor_tensor(
                        varv[:, jj:jj + 1], ssqv[:, jj:jj + 1], RINV, m2v[:, jj:jj + 1],
                        ALU.mult, ALU.subtract))
                self.sec("vector", vln)
                dve_var = dve.n
                def arstd(s, jj=jj, dve_var=dve_var):
                    s.wait_ge(dve.h, dve_var)
                    if dve_eps:
                        s.wait_ge(dve.h, dve_eps)
                    act.tick(s.activation(sdv[:, jj:jj + 1], varv[:, jj:jj + 1],
                                          AF.Sqrt, bias=epsv[:, 0:1]))
                self.sec("scalar", arstd)
                act_rstd = act.n
                def vh1(v, jj=jj, xt=xt, act_rstd=act_rstd):
                    v.wait_ge(act.h, act_rstd)
                    dve.tick(v.reciprocal(rstdv[:, jj:jj + 1], sdv[:, jj:jj + 1]))
                    v.wait_ge(dve.h, dve.n)
                    dve.tick(v.tensor_scalar(xnv[:], xt, mv[:, jj:jj + 1],
                                             rstdv[:, jj:jj + 1],
                                             ALU.subtract, ALU.mult))
                    v.wait_ge(dve.h, dve.n)
                    dve.tick(v.tensor_mul(xnv[:], xnv[:], ca1[:]))
                    v.wait_ge(dve.h, dve.n)
                    dve.tick(v.tensor_add(xt, xnv[:], sa[:]))
                self.sec("vector", vh1)
                dve_h1 = dve.n
                for half in range(2):
                    def ttp(t, jj=jj, half=half, xt=xt, dve_h1=dve_h1):
                        t.wait_ge(dve.h, dve_h1)
                        t.wait_ge(act.h, act.n)
                        for q in range(4):
                            dc = half * 4 + q
                            pe.tick(t.transpose(pstp[half][:, q * 128:(q + 1) * 128],
                                                xt[:, dc * 128:(dc + 1) * 128],
                                                self.idn[:]))
                    self.sec("tensor", ttp)
                    pe_tp = pe.n
                    def atp(s, jj=jj, half=half, hb=hb, pe_tp=pe_tp):
                        s.wait_ge(pe.h, pe_tp)
                        act.tick(s.activation(
                            bass.AP(hb, half * 2048 + jj * 128,
                                    [[4096, NP], [512, 4], [1, 128]]),
                            pstp[half][:], AF.Copy))
                    self.sec("scalar", atp)
            act_h1_all = act.n

            for (wt, stg, dst, bsb) in ((self.wqt, qstg, "qfm", self.bq_sb),
                                        (self.wkt, kstg, "kfm", self.bk_sb)):
                for ic in range(4):
                    def tq(t, wt=wt, ic=ic, hb=hb, act_h1_all=act_h1_all):
                        if ic == 0:
                            t.wait_ge(act.h, act_h1_all)
                        t.wait_ge(act.h, act.n)
                        for dc in range(8):
                            pe.tick(t.matmul(psqk[ic % 2][:],
                                             wt[dc][:, ic * 128:(ic + 1) * 128],
                                             hb[:, dc * 512:(dc + 1) * 512],
                                             start=(dc == 0), stop=(dc == 7)))
                    self.sec("tensor", tq)
                    pe_q = pe.n
                    def aq(s, ic=ic, stg=stg, bsb=bsb, pe_q=pe_q):
                        s.wait_ge(pe.h, pe_q)
                        s.wait_ge(ld.h, ld.n)
                        act.tick(s.activation(stg[:], psqk[ic % 2][:], AF.Identity,
                                              bias=bsb[:, ic:ic + 1]))
                    self.sec("scalar", aq)
                    act_q = act.n
                    def sq_(sync, ic=ic, tb=tb, stg=stg, dst=dst, act_q=act_q):
                        sync.wait_ge(act.h, act_q)
                        ld.tick(sync.dma_start(
                            out=self.dr[dst][ic * 128:(ic + 1) * 128,
                                             tb * 512:(tb + 1) * 512],
                            in_=stg[:]), 16)
                    self.sec("sync", sq_)
            for jj in range(4):
                def tv(t, jj=jj, hb=hb, act_h1_all=act_h1_all):
                    if jj == 0:
                        t.wait_ge(act.h, act_h1_all)
                    t.wait_ge(dve.h, dve.n)
                    for dc in range(8):
                        pe.tick(t.matmul(psqk[jj % 2][:],
                                         hb[:, dc * 512 + jj * 128: dc * 512 + (jj + 1) * 128],
                                         self.wvt[dc][:],
                                         start=(dc == 0), stop=(dc == 7)))
                self.sec("tensor", tv)
                pe_v = pe.n
                def vv(v, jj=jj, pe_v=pe_v):
                    v.wait_ge(pe.h, pe_v)
                    v.wait_ge(ld.h, ld.n)
                    dve.tick(v.tensor_add(vstg[:], psqk[jj % 2][:], self.bvb[:]))
                self.sec("vector", vv)
                dve_v = dve.n
                def sv(sync, jj=jj, tb=tb, dve_v=dve_v):
                    sync.wait_ge(dve.h, dve_v)
                    ld.tick(sync.dma_start(
                        out=self.dr["vtm"][tb * 512 + jj * 128: tb * 512 + (jj + 1) * 128, :],
                        in_=vstg[:]), 16)
                self.sec("sync", sv)
            blk_done[tb] = pe.n

        self.join_all()
        stk.close()
        self.stk_w.close()

    def phase2(self):
        nc, ld, pe, act, dve, pool = self.nc, self.ld, self.pe, self.act, self.dve, self.pool

        stk = ExitStack()
        qf = self.sb("qf", [NP, T], stk=stk)
        kf = self.sb("kf", [NP, T], stk=stk)
        vt = self.sb("vt", [NP, 32 * 130], stk=stk)
        qsq = self.sb("qsq", [NP, T], stk=stk)
        apf = self.sb("apf", [NP, T], stk=stk)
        eb = [self.sb(f"eb{i}", [NP, 512], stk=stk) for i in range(4)]
        zbs = self.sb("zbs", [64, 512], stk=stk)
        rz = self.sb("rz", [1, 512], stk=stk)
        qn = self.sb("qn", [1, T], stk=stk)
        sc1 = self.sb("sc1", [1, 4], stk=stk)
        bvec5 = self.sb("bvec5", [NP, 5], stk=stk)
        mhat_b = bvec5[:, 4:5]
        b30v = [bvec5[:, 0 + i:1 + i] for i in range(2)]
        b0v = [bvec5[:, 2 + i:3 + i] for i in range(2)]
        pp = [self.sb(f"pp{i}", [NP, 256], stk=stk) for i in range(2)]
        psS = [self.ps(f"psS{i}", [NP, 512], stk=stk) for i in range(4)]
        psO = [self.ps(f"psO{i}", [NP, 512], stk=stk) for i in range(2)]
        psZ = self.ps("psZ", [NP, 512], stk=stk)
        psN = self.ps("psN", [1, 512], stk=stk)

        def segs(q5, k):
            c1 = min(max(128 * k - 64 - 512 * q5, 0), 512)
            c2 = min(max(128 * k + 192 - 512 * q5, 0), 512)
            return c1, c2

        for pr in range(NPAIR):
            ppe, pact, pdve = pe.n, act.n, dve.n
            def sload(sync, pr=pr, ppe=ppe, pact=pact, pdve=pdve):
                sync.wait_ge(pe.h, ppe)
                sync.wait_ge(act.h, pact)
                sync.wait_ge(dve.h, pdve)
                sync.wait_ge(ld.h, ld.n)
                ld.tick(sync.dma_start(out=qf[:], in_=self.dr["qfm"][pr * NP:(pr + 1) * NP, :]), 16)
                ld.tick(sync.dma_start(out=kf[:], in_=self.dr["kfm"][pr * NP:(pr + 1) * NP, :]), 16)
                for hl in range(2):
                    ld.tick(sync.dma_start(
                        out=bass.AP(vt, 65 * hl, [[32 * 130, NP], [130, 32], [1, 64]]),
                        in_=bass.AP(self.dr["vtm"].tensor, pr * NP + 64 * hl,
                                    [[512, NP], [512 * NP, 32], [1, 64]])), 16)
            self.sec("sync", sload)
            ld_pair = ld.n
            def vones(v, ppe=ppe):
                v.wait_ge(pe.h, ppe)
                for hl in range(2):
                    dve.tick(v.memset(bass.AP(vt, 64 + 65 * hl,
                                              [[32 * 130, NP], [130, 32], [1, 1]]), 1.0))
            self.sec("vector", vones)
            dve_ones = dve.n

            pe_nb = pe.n
            dve_nb = dve.n
            act_sq1 = act.n + 1
            act_sq2 = act.n + 2
            def a_sq(s, ld_pair=ld_pair, pe_nb=pe_nb):
                s.wait_ge(ld.h, ld_pair)
                act.tick(s.activation(qsq[:], qf[:], AF.Square))
                s.wait_ge(pe.h, pe_nb + 8)
                act.tick(s.activation(qsq[:], kf[:], AF.Square))
            self.sec("scalar", a_sq)
            def t_nb(t, pe_nb=pe_nb, dve_nb=dve_nb, act_sq1=act_sq1, act_sq2=act_sq2):
                for i in range(16):
                    t.wait_ge(act.h, act_sq1 if i < 8 else act_sq2)
                    if i >= 1:
                        t.wait_ge(dve.h, dve_nb + i + (1 if i >= 9 else 0))
                    sg = i % 8
                    pe.tick(t.matmul(psN[:], self.ones_col[:],
                                     qsq[:, sg * 512:(sg + 1) * 512],
                                     start=True, stop=True))
            self.sec("tensor", t_nb)
            def v_nb(v, pe_nb=pe_nb, pr=pr):
                for i in range(16):
                    v.wait_ge(pe.h, pe_nb + i + 1)
                    sg = i % 8
                    dve.tick(v.tensor_copy(qn[0:1, sg * 512:(sg + 1) * 512], psN[0:1, :]))
                    if i == 7:
                        dve.tick(v.tensor_reduce(sc1[0:1, 0:1], qn[0:1, :], AX.X, ALU.max))
            self.sec("vector", v_nb)
            def t_nb2_guard(t):
                pass
            def v_mx(v, pr=pr):
                v.wait_ge(dve.h, dve.n)
                dve.tick(v.tensor_reduce(sc1[0:1, 1:2], qn[0:1, :], AX.X, ALU.max))
                v.wait_ge(dve.h, dve.n)
                dve.tick(v.tensor_add(sc1[0:1, 2:3], sc1[0:1, 0:1], sc1[0:1, 1:2]))
                v.wait_ge(dve.h, dve.n)
                dve.tick(v.tensor_scalar(sc1[0:1, 2:3], sc1[0:1, 2:3], 0.0625,
                                         self.bmaxp[0:1, pr:pr + 1], ALU.mult, ALU.add))
                v.wait_ge(dve.h, dve.n)
                dve.tick(v.tensor_copy(qn[0:1, 4:5], sc1[0:1, 2:3]))
                for hl in range(2):
                    h = 2 * pr + hl
                    dve.tick(v.tensor_scalar(qn[0:1, hl:hl + 1], self.c30row[0:1, h:h + 1],
                                             sc1[0:1, 2:3], None, ALU.subtract))
                    dve.tick(v.tensor_scalar(qn[0:1, 2 + hl:3 + hl], self.c0row[0:1, h:h + 1],
                                             sc1[0:1, 2:3], None, ALU.subtract))
                v.wait_ge(dve.h, dve.n)
            self.sec("vector", v_mx)
            dve_mhat = dve.n
            def tbc2(t, dve_mhat=dve_mhat):
                t.wait_ge(dve.h, dve_mhat)
                pe.tick(t.matmul(psZ[0:128, 0:5], self.ones_row[0:1, 0:128],
                                 qn[0:1, 0:5], start=True, stop=True))
            self.sec("tensor", tbc2)
            pe_bc2 = pe.n
            def vbc2(v, pe_bc2=pe_bc2):
                v.wait_ge(pe.h, pe_bc2)
                dve.tick(v.tensor_copy(bvec5[:], psZ[0:128, 0:5]))
            self.sec("vector", vbc2)
            pool_bc = dve.n
            def vpp(v, pr=pr, pool_bc=pool_bc):
                v.wait_ge(dve.h, pool_bc)
                for hl in range(2):
                    h = 2 * pr + hl
                    dve.tick(v.tensor_scalar(pp[hl][:], self.pat[:, h * 256:(h + 1) * 256],
                                             mhat_b, 8.0, ALU.subtract, ALU.mult))
            self.sec("vector", vpp)
            dve_pp = dve.n

            s_mark, av_mark, act_grp, band_mark = {}, {}, {}, {}
            rz_mark, zb_mark, o_mark = {}, {}, {}

            def walk(emit, T_=None, A_=None, V_=None):
                cpe = [pe.n]; cact = [act.n]; cdve = [dve.n]
                def tick(c, real, instr=None, k=1):
                    if not emit:
                        c[0] += k
                        return c[0]
                    real.tick(instr, k)
                    return real.n
                for q5 in range(8):
                    def pe_stream(t=None):
                        for k in range(32):
                            for hl in range(2):
                                if emit:
                                    if k >= 2:
                                        t.wait_ge(act.h, act_grp[(q5, k - 2, hl)])
                                    elif q5 > 0:
                                        t.wait_ge(act.h, act_grp[(q5 - 1, 30 + k, hl)])
                                    elif q5 == 0 and k < 2:
                                        t.wait_ge(ld.h, ld_pair)
                                        t.wait_ge(dve.h, max(dve_ones, dve_pp))
                                    m = tick(cpe, pe, t.matmul(
                                        psS[hl * 2 + k % 2][:],
                                        kf[64 * hl:64 * hl + 64, 128 * k:128 * (k + 1)],
                                        qf[64 * hl:64 * hl + 64, 512 * q5:512 * (q5 + 1)],
                                        start=True, stop=True,
                                        tile_position=(64 * hl, 0)))
                                else:
                                    m = tick(cpe, pe)
                            s_mark[(q5, k)] = m
                            for hl in range(2):
                                if emit:
                                    if k == 0 and q5 > 0:
                                        t.wait_ge(dve.h, o_mark[(q5 - 1, hl)])
                                    t.wait_ge(act.h, act_grp[(q5, k, hl)])
                                    m = tick(cpe, pe, t.matmul(
                                        psO[hl][0:65, :],
                                        vt[:, 130 * k + 65 * hl:130 * k + 65 * hl + 65],
                                        eb[hl * 2 + k % 2][:],
                                        start=(k == 0), stop=(k == 31)))
                                else:
                                    m = tick(cpe, pe)
                                av_mark[(q5, k, hl)] = m
                        for hl in range(2):
                            if emit:
                                t.wait_ge(dve.h, rz_mark[(q5, hl)])
                                m = tick(cpe, pe, t.matmul(
                                    psZ[0:64, :], self.ones_row[0:1, 0:64],
                                    rz[0:1, :], start=True, stop=True))
                            else:
                                m = tick(cpe, pe)
                            zb_mark[(q5, hl)] = m
                    def act_stream(s=None):
                        for k in range(32):
                            c1, c2 = segs(q5, k)
                            for hl in range(2):
                                if emit:
                                    s.wait_ge(pe.h, s_mark[(q5, k)])
                                    if k >= 2:
                                        s.wait_ge(pe.h, av_mark[(q5, k - 2, hl)])
                                    elif q5 > 0:
                                        s.wait_ge(pe.h, av_mark[(q5 - 1, 30 + k, hl)])
                                    if c2 > c1:
                                        s.wait_ge(dve.h, band_mark[(q5, k, hl)])
                                bank = psS[hl * 2 + k % 2]
                                ebuf = eb[hl * 2 + k % 2]
                                m = None
                                if c1 > 0:
                                    m = tick(cact, act, None if not emit else s.activation(
                                        ebuf[:, 0:c1], bank[:, 0:c1], AF.Exp,
                                        bias=b30v[hl], scale=0.125))
                                if c2 > c1:
                                    m = tick(cact, act, None if not emit else s.activation(
                                        ebuf[:, c1:c2], bank[:, c1:c2], AF.Exp,
                                        bias=0.0, scale=0.125))
                                if c2 < 512:
                                    m = tick(cact, act, None if not emit else s.activation(
                                        ebuf[:, c2:512], bank[:, c2:512], AF.Exp,
                                        bias=b0v[hl], scale=0.125))
                                act_grp[(q5, k, hl)] = m
                    def dve_stream(v=None):
                        for k in range(32):
                            c1, c2 = segs(q5, k)
                            if c2 <= c1:
                                continue
                            for hl in range(2):
                                if emit:
                                    v.wait_ge(pe.h, s_mark[(q5, k)])
                                    j0 = 512 * q5 + c1 - 128 * k + 64
                                    bank = psS[hl * 2 + k % 2]
                                    m = tick(cdve, dve, v.scalar_tensor_tensor(
                                        bank[:, c1:c2], bank[:, c1:c2], 1.0,
                                        pp[hl][:, j0:j0 + (c2 - c1)],
                                        ALU.mult, ALU.add))
                                else:
                                    m = tick(cdve, dve)
                                band_mark[(q5, k, hl)] = m
                        for hl in range(2):
                            if emit:
                                v.wait_ge(pe.h, av_mark[(q5, 31, hl)])
                                m = tick(cdve, dve, v.reciprocal(rz[0:1, :], psO[hl][64:65, :]))
                            else:
                                m = tick(cdve, dve)
                            rz_mark[(q5, hl)] = m
                            if emit:
                                v.wait_ge(pe.h, zb_mark[(q5, hl)])
                                tick(cdve, dve, v.tensor_copy(zbs[:], psZ[0:64, :]))
                                v.wait_ge(dve.h, cdve[0])
                                m = tick(cdve, dve, v.tensor_mul(
                                    apf[64 * hl:64 * hl + 64, 512 * q5:512 * (q5 + 1)],
                                    psO[hl][0:64, :], zbs[:]))
                            else:
                                tick(cdve, dve)
                                m = tick(cdve, dve)
                            o_mark[(q5, hl)] = m
                    if emit:
                        T_(q5, pe_stream); A_(q5, act_stream); V_(q5, dve_stream)
                    else:
                        pe_stream(); act_stream(); dve_stream()

            walk(False)
            def T_(q5, f):
                self.sec("tensor", lambda t: f(t))
            def A_(q5, f):
                self.sec("scalar", lambda s: f(s))
            def V_(q5, f):
                self.sec("vector", lambda v: f(v))
            walk(True, T_, A_, V_)

            def spair(sync, pr=pr):
                sync.wait_ge(dve.h, dve.n)
                ld.tick(sync.dma_start(out=self.dr["afm"][pr * NP:(pr + 1) * NP, :],
                                       in_=apf[:]), 16)
            self.sec("sync", spair)

        self.join_all()
        stk.close()

    def phase3(self):
        nc, ld, pe, act, dve, pool = self.nc, self.ld, self.pe, self.act, self.dve, self.pool
        gd, cc = self.gd, self.cc

        stk = ExitStack()
        a4 = [self.sb(f"a4_{i}", [NP, T], stk=stk) for i in range(4)]
        wot = [self.sb(f"wot{i}", [NP, D], stk=stk) for i in range(4)]
        ostg = self.sb("ostg", [NP, D], stk=stk)
        otile = self.sb("otile", [NP, D], stk=stk)
        xot = self.sb("xot", [NP, D], stk=stk)
        x2t = self.sb("x2t", [NP, D], stk=stk)
        xnv3 = self.sb("xnv3", [NP, D], stk=stk)
        h2t = self.sb("h2t", [NP, D], stk=stk)
        h2fblk = [self.sb(f"h2f{i}", [NP, 512], stk=stk) for i in range(8)]
        lgstg = self.sb("lgstg", [NP, 8], stk=stk)
        ln1 = self.sb("ln1", [NP, 8], stk=stk)
        eps3 = self.sb("eps3", [NP, 1], stk=stk)
        pso = [self.ps(f"pso{i}", [NP, 512], stk=stk) for i in range(2)]
        pstp3 = [self.ps(f"pstp3_{i}", [NP, 512], stk=stk) for i in range(2)]
        pslg = self.ps("pslg", [NP, 8], stk=stk)

        cb1, sb_, ga, bob = self.mod["cb1"], self.mod["sb"], self.mod["ga"], self.bob

        def s30(sync):
            for i in range(4):
                ld.tick(sync.dma_start(out=a4[i][:], in_=self.dr["afm"][i * NP:(i + 1) * NP, :]), 16)
                ld.tick(sync.dma_start(out=wot[i][:], in_=self.dr["wto"][i * NP:(i + 1) * NP, :]), 16)
        self.sec("sync", s30)
        ld_a = ld.n
        def vep3(v):
            dve.tick(v.memset(eps3[:], LNEPS))
        self.sec("vector", vep3)
        dve_eps3 = dve.n

        evac_mark = {}
        for tt in range(32):
            for jb in range(2):
                def tproj(t, tt=tt, jb=jb, ld_a=ld_a):
                    if tt == 0 and jb == 0:
                        t.wait_ge(ld.h, ld_a)
                    idx = (tt * 2 + jb)
                    if idx >= 2:
                        t.wait_ge(act.h, evac_mark[idx - 2])
                    for vc in range(4):
                        pe.tick(t.matmul(pso[jb][:],
                                         a4[vc][:, 128 * tt:128 * (tt + 1)],
                                         wot[vc][:, 512 * jb:512 * (jb + 1)],
                                         start=(vc == 0), stop=(vc == 3)))
                self.sec("tensor", tproj)
                pe_pr = pe.n
                def aev(s, tt=tt, jb=jb, pe_pr=pe_pr):
                    s.wait_ge(pe.h, pe_pr)
                    if tt > 0 or jb > 0:
                        s.wait_ge(ld.h, ld.n)
                    act.tick(s.activation(ostg[:, 512 * jb:512 * (jb + 1)],
                                          pso[jb][:], AF.Copy))
                    evac_mark[tt * 2 + jb] = act.n
                self.sec("scalar", aev)
            def sst(sync, tt=tt):
                sync.wait_ge(act.h, act.n)
                ld.tick(sync.dma_start(out=self.dr["rsin"][128 * tt:128 * (tt + 1), :],
                                       in_=ostg[:]), 16)
            self.sec("sync", sst)
        ld_rsin = ld.n

        def grs(g, ld_rsin=ld_rsin):
            g.wait_ge(ld.h, ld_rsin)
            cc.tick(g.collective_compute(
                "ReduceScatter", ALU.add,
                replica_groups=[[0, 1], [2, 3], [4, 5], [6, 7]],
                ins=[self.dr["rsin"][:, :]], outs=[self.dr["rsout"][:, :]]))
        self.sec("gpsimd", grs)
        cc_rs = cc.n

        sm, ssq = ln1[:, 0:1], ln1[:, 1:2]
        mvv, m2 = ln1[:, 2:3], ln1[:, 3:4]
        var, sd, rstd = ln1[:, 4:5], ln1[:, 5:6], ln1[:, 6:7]
        for tt2 in range(16):
            blk, sl = tt2 // 4, tt2 % 4
            def sld2(sync, tt2=tt2, cc_rs=cc_rs):
                if tt2 == 0:
                    sync.wait_ge(cc.h, cc_rs)
                sync.wait_ge(dve.h, dve.n)
                sync.wait_ge(pe.h, pe.n)
                ld.tick(sync.dma_start(out=otile[:],
                                       in_=self.dr["rsout"][128 * tt2:128 * (tt2 + 1), :]), 16)
                ld.tick(sync.dma_start(out=xot[:],
                                       in_=self.inp["xo"][128 * tt2:128 * (tt2 + 1), :]), 16)
            self.sec("sync", sld2)
            ld_t = ld.n
            def vcmb(v, ld_t=ld_t):
                v.wait_ge(ld.h, ld_t)
                dve.tick(v.tensor_add(x2t[:], otile[:], bob[:]))
                v.wait_ge(dve.h, dve.n)
                dve.tick(v.tensor_mul(x2t[:], x2t[:], ga[:]))
                v.wait_ge(dve.h, dve.n)
                dve.tick(v.tensor_add(x2t[:], x2t[:], xot[:]))
                v.wait_ge(dve.h, dve.n)
                dve.tick(v.reduce_sum(sm, x2t[:], AX.X))
            self.sec("vector", vcmb)
            dve_x2 = dve.n
            def aq2(s, dve_x2=dve_x2):
                s.wait_ge(dve.h, dve_x2)
                act.tick(s.activation(xnv3[:], x2t[:], AF.Square, accum_out=ssq))
            self.sec("scalar", aq2)
            act_sq = act.n
            def vvar(v, act_sq=act_sq):
                v.wait_ge(act.h, act_sq)
                dve.tick(v.tensor_scalar_mul(mvv, sm, RINV))
                v.wait_ge(dve.h, dve.n)
                dve.tick(v.scalar_tensor_tensor(m2, mvv, 1.0, mvv, ALU.mult, ALU.mult))
                v.wait_ge(dve.h, dve.n)
                dve.tick(v.scalar_tensor_tensor(var, ssq, RINV, m2, ALU.mult, ALU.subtract))
            self.sec("vector", vvar)
            dve_var = dve.n
            def asd(s, dve_var=dve_var):
                s.wait_ge(dve.h, dve_var)
                s.wait_ge(dve.h, dve_eps3)
                act.tick(s.activation(sd, var, AF.Sqrt, bias=eps3[:, 0:1]))
            self.sec("scalar", asd)
            act_sd = act.n
            def vh2(v, act_sd=act_sd):
                v.wait_ge(act.h, act_sd)
                dve.tick(v.reciprocal(rstd, sd))
                v.wait_ge(dve.h, dve.n)
                dve.tick(v.tensor_scalar(xnv3[:], x2t[:], mvv, rstd, ALU.subtract, ALU.mult))
                v.wait_ge(dve.h, dve.n)
                dve.tick(v.tensor_mul(xnv3[:], xnv3[:], cb1[:]))
                v.wait_ge(dve.h, dve.n)
                dve.tick(v.tensor_add(h2t[:], xnv3[:], sb_[:]))
            self.sec("vector", vh2)
            dve_h2 = dve.n
            def sst2(sync, tt2=tt2, dve_x2=dve_x2, dve_h2=dve_h2):
                sync.wait_ge(dve.h, dve_x2)
                ld.tick(sync.dma_start(out=self.dr["x2d"][128 * tt2:128 * (tt2 + 1), :],
                                       in_=x2t[:]), 16)
                sync.wait_ge(dve.h, dve_h2)
                ld.tick(sync.dma_start(out=self.dr["h2tm"][128 * tt2:128 * (tt2 + 1), :],
                                       in_=h2t[:]), 16)
            self.sec("sync", sst2)
            for half in range(2):
                def ttp2(t, half=half, dve_h2=dve_h2):
                    t.wait_ge(dve.h, dve_h2)
                    t.wait_ge(act.h, act.n)
                    for q in range(4):
                        dc = half * 4 + q
                        pe.tick(t.transpose(pstp3[half][:, q * 128:(q + 1) * 128],
                                            h2t[:, dc * 128:(dc + 1) * 128],
                                            self.idn[:]))
                self.sec("tensor", ttp2)
                pe_tp = pe.n
                def atp2(s, half=half, sl=sl, pe_tp=pe_tp):
                    s.wait_ge(pe.h, pe_tp)
                    for q in range(4):
                        dc = half * 4 + q
                        act.tick(s.activation(h2fblk[dc][:, sl * 128:(sl + 1) * 128],
                                              pstp3[half][:, q * 128:(q + 1) * 128],
                                              AF.Copy))
                self.sec("scalar", atp2)
            act_h2f = act.n
            if sl == 3:
                for t128 in range(4):
                    def trt(t, blk=blk, t128=t128, act_h2f=act_h2f):
                        t.wait_ge(act.h, act_h2f)
                        if t128 >= 1:
                            t.wait_ge(dve.h, dve.n)
                        for dc in range(8):
                            pe.tick(t.matmul(pslg[:],
                                             h2fblk[dc][:, t128 * 128:(t128 + 1) * 128],
                                             self.wrt[:, dc * 8:(dc + 1) * 8],
                                             start=(dc == 0), stop=(dc == 7)))
                    self.sec("tensor", trt)
                    pe_lg = pe.n
                    def vlg(v, pe_lg=pe_lg):
                        v.wait_ge(pe.h, pe_lg)
                        v.wait_ge(ld.h, ld.n)
                        dve.tick(v.tensor_copy(lgstg[:], pslg[:]))
                    self.sec("vector", vlg)
                    dve_lg = dve.n
                    def slg(sync, blk=blk, t128=t128, dve_lg=dve_lg):
                        sync.wait_ge(dve.h, dve_lg)
                        ld.tick(sync.dma_start(
                            out=self.dr["lgin"][blk * 512 + t128 * 128: blk * 512 + (t128 + 1) * 128, :],
                            in_=lgstg[:]), 16)
                    self.sec("sync", slg)
                def sh2f(sync, blk=blk, act_h2f=act_h2f):
                    sync.wait_ge(act.h, act_h2f)
                    for dc in range(8):
                        ld.tick(sync.dma_start(
                            out=self.dr["h2fm"][dc * NP:(dc + 1) * NP, blk * 512:(blk + 1) * 512],
                            in_=h2fblk[dc][:]), 16)
                self.sec("sync", sh2f)

        ld_lg = ld.n
        def gag(g, ld_lg=ld_lg):
            g.wait_ge(ld.h, ld_lg)
            cc.tick(g.collective_compute(
                "AllGather", ALU.bypass,
                replica_groups=[[0, 1, 2, 3, 4, 5, 6, 7]],
                ins=[self.dr["lgin"][:, :]], outs=[self.dr["lgout"][:, :]]))
        self.sec("gpsimd", gag)
        self.cc_ag = cc.n

        self.join_all()
        stk.close()

    def phase4(self):
        nc, ld, pe, act, dve, pool = self.nc, self.ld, self.pe, self.act, self.dve, self.pool
        gd, cc = self.gd, self.cc

        self.stkA.close()
        stkR = ExitStack()
        ind_idx = {k: self.sb("ind_idx_" + k, [NP, 16], I32, stk=stkR) for k in
                   ("g1", "g2", "y1", "y2")}
        csl = self.sb("csl", [1, 1], I32, stk=stkR)
        zeros = self.sb("zeros", [NP, 1024], stk=stkR)
        wf16 = {k: self.sb("wf16_" + k, [NP, 16], stk=stkR) for k in ("w1", "w2")}
        stk4a = ExitStack()
        LG = self.sb("LG", [NP, 1024], stk=stk4a)
        bc8 = self.sb("bc8", [NP, 1024], stk=stk4a)
        tmp8 = self.sb("tmp8", [NP, 1024], stk=stk4a)
        oh1 = self.sb("oh1", [NP, 1024], stk=stk4a)
        oh2 = self.sb("oh2", [NP, 1024], stk=stk4a)
        m1 = self.sb("m1", [NP, 128], stk=stk4a)
        m2 = self.sb("m2", [NP, 128], stk=stk4a)
        i1r = self.sb("i1r", [NP, 128], stk=stk4a)
        w1m = self.sb("w1m", [NP, 128], stk=stk4a)
        w2m = self.sb("w2m", [NP, 128], stk=stk4a)
        ohc1 = self.sb("ohc1", [NP, 8 * 128], stk=stk4a)
        ohc2 = self.sb("ohc2", [NP, 8 * 128], stk=stk4a)
        w1c = self.sb("w1c", [NP, 128], stk=stk4a)
        w2c = self.sb("w2c", [NP, 128], stk=stk4a)
        fbc = self.sb("fbc", [NP, 128], stk=stk4a)
        elig = self.sb("elig", [NP, 128], stk=stk4a)
        kept = self.sb("kept", [NP, 128], stk=stk4a)
        krx = self.sb("krx", [NP, 128], stk=stk4a)
        rs1 = self.sb("rs1", [NP, 128], stk=stk4a)
        rs2 = self.sb("rs2", [NP, 128], stk=stk4a)
        trow = self.sb("trow", [1, 128], stk=stk4a)
        tcol = self.sb("tcol", [NP, 1], stk=stk4a)
        orow = self.sb("orow", [1, 128], stk=stk4a)
        nbb = self.sb("nbb", [NP, 1], stk=stk4a)
        ms = {k: self.sb("ms_" + k, [NP, 128], stk=stk4a) for k in
              ("kl", "slot", "slotc", "stgt", "t1", "t2", "t3", "t4",
               "s1a", "s2a", "y1a", "y2a", "w1a", "w2a", "k1a", "k2a", "fbm",
               "u1", "u2", "w1f", "w2f", "d1", "d2", "dfb", "yb1", "yb2")}
        cvt = self.sb("cvt", [NP, 128], I32, stk=stk4a)
        psC = self.ps("psC", [NP, 128], stk=stk4a)
        psTB = self.ps("psTB", [NP, 512], stk=stk4a)
        psT4 = psTB[:, 0:128]
        psB4 = psTB[:, 128:256]

        def s40(sync):
            ld.tick(sync.dma_start(out=csl[:], in_=self.inp["cslot"][:, :]), 16)
        self.sec("sync", s40)
        ld_s40 = ld.n
        def v40(v):
            dve.tick(v.memset(zeros[:], 0.0))
            dve.tick(v.memset(fbc[:], 0.0))
            for k in ("s1a", "s2a", "y1a", "y2a", "w1a", "w2a", "k1a", "k2a"):
                dve.tick(v.memset(ms[k][:], 0.0))
            for k in ind_idx:
                dve.tick(v.memset(ind_idx[k][:], 0))
        self.sec("vector", v40)
        dve_v40 = dve.n

        def gz(g, dve_v40=dve_v40):
            g.wait_ge(dve.h, dve_v40)
            for r0 in range(0, GROWS, NP):
                gd.tick(g.dma_start(out=self.dr["gbuf"][r0:r0 + NP, :], in_=zeros[:]), 16)
        self.sec("gpsimd", gz)
        gd_zero = gd.n

        stkF = ExitStack()
        h2fb = [self.sb(f"h2fb{i}", [NP, 512], stk=stkF) for i in range(8)]
        fwblk = [self.sb(f"fwblk{i}", [NP, 8 * 512], stk=stkF) for i in range(2)]
        fw2t = [self.sb(f"fw2t{i}", [NP, 512], stk=stkF) for i in range(2)]
        a1f = [self.sb(f"a1f{i}", [NP, 512], stk=stkF) for i in range(32)]
        fb1c = self.sb("fb1c", [NP, 32], stk=stkF)
        fb2b = self.sb("fb2b", [NP, D], stk=stkF)
        brow = self.sb("brow", [1, D], stk=stkF)
        ystg4 = [self.sb(f"ystg4_{i}", [NP, 512], stk=stkF) for i in range(2)]
        psf = [self.ps(f"psf{i}", [NP, 512], stk=stkF) for i in range(2)]
        psF2 = [self.ps(f"psF2_{i}", [NP, 512], stk=stkF) for i in range(4)]

        def s41(sync):
            ld.tick(sync.dma_start(out=fb1c[:], in_=bass.AP(
                self.inp["fb1"].tensor, 0, [[1, NP], [NP, 32]]),
                allow_slow_non_contiguous=True), 16)
            ld.tick(sync.dma_start(out=brow[0:1, :], in_=self.inp["fb2"][:]
                                   .rearrange("(a d) -> a d", a=1)), 16)
        self.sec("sync", s41)
        ld_s41 = ld.n
        def t41(t, ld_s41=ld_s41):
            t.wait_ge(ld.h, ld_s41)
            for sgb in range(2):
                t.wait_ge(act.h, act.n)
                pe.tick(t.matmul(psf[sgb][:], self.ones_row[0:1, 0:128],
                                 brow[0:1, sgb * 512:(sgb + 1) * 512],
                                 start=True, stop=True))
        self.sec("tensor", t41)
        pe_fb2 = pe.n
        def a41(s, pe_fb2=pe_fb2):
            s.wait_ge(pe.h, pe_fb2)
            for sgb in range(2):
                act.tick(s.activation(fb2b[:, sgb * 512:(sgb + 1) * 512],
                                      psf[sgb][:], AF.Copy))
        self.sec("scalar", a41)
        pool_fb2 = pool.n

        for tb in range(4):
            pe_tb = pe.n
            act_tb = act.n
            def sh2l(sync, tb=tb, pe_tb=pe_tb):
                if tb > 0:
                    sync.wait_ge(pe.h, pe_tb)
                for dc in range(8):
                    ld.tick(sync.dma_start(
                        out=h2fb[dc][:],
                        in_=self.dr["h2fm"][dc * NP:(dc + 1) * NP, tb * 512:(tb + 1) * 512]), 16)
            self.sec("sync", sh2l)
            ld_h2 = ld.n
            for ibk in range(8):
                def sfw1(sync, ibk=ibk, pe_tb=pe_tb):
                    if ibk >= 2:
                        sync.wait_ge(pe.h, pe_tb + 32 * (ibk - 1))
                    for dc in range(8):
                        ld.tick(sync.dma_start(
                            out=fwblk[ibk % 2][:, dc * 512:(dc + 1) * 512],
                            in_=bass.AP(self.inp["fw1"].tensor,
                                        dc * NP * DI + ibk * 512,
                                        [[DI, NP], [1, 512]])), 16)
                self.sec("sync", sfw1)
                ld_fw = ld.n
                def tmm1(t, ibk=ibk, ld_fw=ld_fw, ld_h2=ld_h2, act_tb=act_tb):
                    t.wait_ge(ld.h, max(ld_fw, ld_h2))
                    for ic2 in range(4):
                        ic = ibk * 4 + ic2
                        if ic >= 2:
                            t.wait_ge(act.h, act_tb + ic - 1)
                        for dc in range(8):
                            pe.tick(t.matmul(
                                psf[ic % 2][:],
                                fwblk[ibk % 2][:, dc * 512 + ic2 * 128: dc * 512 + (ic2 + 1) * 128],
                                h2fb[dc][:],
                                start=(dc == 0), stop=(dc == 7)))
                self.sec("tensor", tmm1)
                def agel(s, ibk=ibk, pe_tb=pe_tb, ld_s41=ld_s41):
                    s.wait_ge(ld.h, ld_s41)
                    for ic2 in range(4):
                        ic = ibk * 4 + ic2
                        s.wait_ge(pe.h, pe_tb + 32 * ibk + 8 * (ic2 + 1))
                        act.tick(s.activation(a1f[ic][:], psf[ic % 2][:], AF.Gelu,
                                              bias=fb1c[:, ic:ic + 1]))
                self.sec("scalar", agel)
            act_gel_all = act.n
            for ob in range(2):
                pe_ob = pe.n
                dve_ev_prev = dve.n
                for ic in range(32):
                    def sfw2(sync, ic=ic, ob=ob, pe_ob=pe_ob):
                        if ic >= 2:
                            sync.wait_ge(pe.h, pe_ob + 4 * (ic - 1))
                        ld.tick(sync.dma_start(
                            out=fw2t[ic % 2][:],
                            in_=bass.AP(self.inp["fw2"].tensor, ic * NP * D + ob * 512,
                                        [[D, NP], [1, 512]])), 16)
                    self.sec("sync", sfw2)
                    ld_f2 = ld.n
                    def tmm2(t, ic=ic, ld_f2=ld_f2, act_gel_all=act_gel_all,
                             dve_ev_prev=dve_ev_prev):
                        t.wait_ge(ld.h, ld_f2)
                        if ic == 0:
                            t.wait_ge(act.h, act_gel_all)
                            t.wait_ge(dve.h, dve_ev_prev)
                        for tt3 in range(4):
                            pe.tick(t.matmul(psF2[tt3][:],
                                             a1f[ic][:, tt3 * 128:(tt3 + 1) * 128],
                                             fw2t[ic % 2][:],
                                             start=(ic == 0), stop=(ic == 31)))
                    self.sec("tensor", tmm2)
                pe_mm2 = pe.n
                for tt3 in range(4):
                    def vev2(v, ob=ob, tt3=tt3, pe_mm2=pe_mm2):
                        v.wait_ge(pe.h, pe_mm2)
                        if tt3 >= 2:
                            v.wait_ge(ld.h, ld.n)
                        dve.tick(v.tensor_add(ystg4[tt3 % 2][:], psF2[tt3][:],
                                              fb2b[:, ob * 512:(ob + 1) * 512]))
                    self.sec("vector", vev2)
                    dve_ev = dve.n
                    def sst4(sync, tb=tb, ob=ob, tt3=tt3, dve_ev=dve_ev):
                        sync.wait_ge(dve.h, dve_ev)
                        r0 = FBB + tb * 512 + tt3 * 128
                        ld.tick(sync.dma_start(
                            out=self.dr["ybuf"][r0:r0 + NP, ob * 512:(ob + 1) * 512],
                            in_=ystg4[tt3 % 2][:]), 16)
                    self.sec("sync", sst4)

        def sld_lg(sync):
            sync.wait_ge(cc.h, self.cc_ag)
            ld.tick(sync.dma_start(out=LG[:], in_=bass.AP(
                self.dr["lgout"].tensor, 0, [[1024, NP], [1, 1024]])), 16)
        self.sec("sync", sld_lg)
        ld_lg2 = ld.n

        def bc_into(v, dst, srcsm):
            for e in range(8):
                dve.tick(v.tensor_copy(
                    bass.AP(dst, e, [[1024, NP], [8, 128]]), srcsm[:]))

        def vtop(v, ld_lg2=ld_lg2, ld_s40=ld_s40, dve_v40=dve_v40):
            v.wait_ge(ld.h, max(ld_lg2, ld_s40))
            v.wait_ge(dve.h, dve_v40)
            dve.tick(v.tensor_reduce(m1[:], LG[:].rearrange("p (j e) -> p j e", e=8),
                                     AX.X, ALU.max))
            v.wait_ge(dve.h, dve.n)
            bc_into(v, bc8, m1)
            v.wait_ge(dve.h, dve.n)
            dve.tick(v.tensor_tensor(tmp8[:], LG[:], bc8[:], ALU.is_equal))
            v.wait_ge(dve.h, dve.n)
            dve.tick(v.tensor_mul(tmp8[:], tmp8[:], self.rev8[:]))
            v.wait_ge(dve.h, dve.n)
            dve.tick(v.tensor_reduce(i1r[:], tmp8[:].rearrange("p (j e) -> p j e", e=8),
                                     AX.X, ALU.max))
            v.wait_ge(dve.h, dve.n)
            bc_into(v, bc8, i1r)
            v.wait_ge(dve.h, dve.n)
            dve.tick(v.tensor_tensor(oh1[:], self.rev8[:], bc8[:], ALU.is_equal))
            v.wait_ge(dve.h, dve.n)
            dve.tick(v.tensor_scalar_mul(tmp8[:], oh1[:], 1.0e30))
            v.wait_ge(dve.h, dve.n)
            dve.tick(v.tensor_sub(tmp8[:], LG[:], tmp8[:]))
            v.wait_ge(dve.h, dve.n)
            dve.tick(v.tensor_reduce(m2[:], tmp8[:].rearrange("p (j e) -> p j e", e=8),
                                     AX.X, ALU.max))
            v.wait_ge(dve.h, dve.n)
            bc_into(v, bc8, m2)
            v.wait_ge(dve.h, dve.n)
            dve.tick(v.tensor_tensor(bc8[:], tmp8[:], bc8[:], ALU.is_equal))
            v.wait_ge(dve.h, dve.n)
            dve.tick(v.tensor_mul(bc8[:], bc8[:], self.rev8[:]))
            v.wait_ge(dve.h, dve.n)
            dve.tick(v.tensor_reduce(i1r[:], bc8[:].rearrange("p (j e) -> p j e", e=8),
                                     AX.X, ALU.max))
            v.wait_ge(dve.h, dve.n)
            bc_into(v, bc8, i1r)
            v.wait_ge(dve.h, dve.n)
            dve.tick(v.tensor_tensor(oh2[:], self.rev8[:], bc8[:], ALU.is_equal))
            v.wait_ge(dve.h, dve.n)
            dve.tick(v.tensor_sub(i1r[:], m1[:], m2[:]))
        self.sec("vector", vtop)
        dve_top = dve.n
        def asig(s, dve_top=dve_top):
            s.wait_ge(dve.h, dve_top)
            act.tick(s.activation(w1m[:], i1r[:], AF.Sigmoid))
            act.tick(s.activation(w2m[:], i1r[:], AF.Sigmoid, scale=-1.0))
        self.sec("scalar", asig)
        act_sig = act.n

        tp_jobs = [(oh1, ohc1, 0), (oh2, ohc2, 1)]
        for e in range(8):
            for (srcT, dstT, _i) in tp_jobs:
                def ttpe(t, srcT=srcT, e=e, act_sig=act_sig):
                    t.wait_ge(act.h, max(act_sig, act.n))
                    pe.tick(t.transpose(psT4,
                                        bass.AP(srcT, e, [[1024, NP], [8, 128]]),
                                        self.idn[:]))
                self.sec("tensor", ttpe)
                pe_t = pe.n
                def aev4(s, dstT=dstT, e=e, pe_t=pe_t):
                    s.wait_ge(pe.h, pe_t)
                    act.tick(s.activation(dstT[:, e * 128:(e + 1) * 128], psT4, AF.Copy))
                self.sec("scalar", aev4)
        for (srcT, dstT) in ((w1m, w1c), (w2m, w2c)):
            def ttpw(t, srcT=srcT):
                t.wait_ge(act.h, act.n)
                pe.tick(t.transpose(psT4, srcT[:], self.idn[:]))
            self.sec("tensor", ttpw)
            pe_t = pe.n
            def aevw(s, dstT=dstT, pe_t=pe_t):
                s.wait_ge(pe.h, pe_t)
                act.tick(s.activation(dstT[:], psT4, AF.Copy))
            self.sec("scalar", aevw)
        act_cm = act.n

        def vmy(v, act_cm=act_cm):
            v.wait_ge(act.h, act_cm)
            dve.tick(v.tensor_scalar_add(ms["d1"][:], self.tloc[:], float(GD1)))
            dve.tick(v.tensor_scalar_add(ms["d2"][:], self.tloc[:], float(GD2)))
            dve.tick(v.tensor_scalar_add(ms["dfb"][:], self.tloc[:], float(FBB)))
        self.sec("vector", vmy)

        for e in range(8):
            def vel(v, e=e):
                v.wait_ge(dve.h, dve.n)
                dve.tick(v.tensor_add(elig[:], ohc1[:, e * 128:(e + 1) * 128],
                                      ohc2[:, e * 128:(e + 1) * 128]))
                dve.tick(v.tensor_scalar(rs2[:], fbc[:], -1.0, 1.0, ALU.mult, ALU.add))
                v.wait_ge(dve.h, dve.n)
                dve.tick(v.tensor_mul(elig[:], elig[:], rs2[:]))
            self.sec("vector", vel)

            def cumsum_chain(srctens):
                def tcs(t, dve_el=dve.n):
                    t.wait_ge(dve.h, dve_el)
                    pe.tick(t.matmul(psC[:], self.Ltri[:], srctens[:],
                                     start=True, stop=True))
                    pe.tick(t.matmul(psTB[0:1, 256:384], self.ones_col[:],
                                     srctens[:], start=True, stop=True))
                self.sec("tensor", tcs)
                pe_cs = pe.n
                def vtr(v, pe_cs=pe_cs):
                    v.wait_ge(pe.h, pe_cs)
                    dve.tick(v.tensor_copy(trow[:], psTB[0:1, 256:384]))
                self.sec("vector", vtr)
                dve_tr = dve.n
                def ttc(t, dve_tr=dve_tr):
                    t.wait_ge(dve.h, dve_tr)
                    pe.tick(t.transpose(psTB[0:128, 0:1], trow[:], self.idn[0:1, 0:1]))
                self.sec("tensor", ttc)
                pe_tc = pe.n
                def vtc(v, pe_tc=pe_tc):
                    v.wait_ge(pe.h, pe_tc)
                    dve.tick(v.tensor_copy(tcol[:], psTB[0:128, 0:1]))
                self.sec("vector", vtc)
                dve_tc = dve.n
                def tof(t, dve_tc=dve_tc):
                    t.wait_ge(dve.h, dve_tc)
                    pe.tick(t.matmul(psB4[0:1, 0:128], tcol[:], self.Lstr[:],
                                     start=True, stop=True))
                self.sec("tensor", tof)
                pe_of = pe.n
                def vor(v, pe_of=pe_of):
                    v.wait_ge(pe.h, pe_of)
                    dve.tick(v.tensor_copy(orow[:], psB4[0:1, 0:128]))
                self.sec("vector", vor)
                dve_or = dve.n
                def tob(t, dve_or=dve_or):
                    t.wait_ge(dve.h, dve_or)
                    pe.tick(t.matmul(psB4, self.ones_row[0:1, 0:128], orow[:],
                                     start=True, stop=True))
                self.sec("tensor", tob)
                return pe.n

            pe_ob4 = cumsum_chain(elig)
            def vrank(v, pe_ob4=pe_ob4):
                v.wait_ge(pe.h, pe_ob4)
                dve.tick(v.tensor_sub(rs1[:], psC[:], elig[:]))
                v.wait_ge(dve.h, dve.n)
                dve.tick(v.tensor_add(rs1[:], rs1[:], psB4))
                v.wait_ge(dve.h, dve.n)
                dve.tick(v.tensor_scalar(rs2[:], rs1[:], float(CAP), None, ALU.is_lt))
                v.wait_ge(dve.h, dve.n)
                dve.tick(v.tensor_mul(kept[:], rs2[:], elig[:]))
                v.wait_ge(dve.h, dve.n)
                dve.tick(v.tensor_sub(rs2[:], elig[:], kept[:]))
                v.wait_ge(dve.h, dve.n)
                dve.tick(v.tensor_add(fbc[:], fbc[:], rs2[:]))
            self.sec("vector", vrank)
            pe_ob5 = cumsum_chain(kept)
            def vkr0(v, pe_ob5=pe_ob5):
                v.wait_ge(pe.h, pe_ob5)
                dve.tick(v.tensor_sub(krx[:], psC[:], kept[:]))
                v.wait_ge(dve.h, dve.n)
                dve.tick(v.tensor_add(krx[:], krx[:], psB4))
                v.wait_ge(dve.h, dve.n)
                dve.tick(v.tensor_copy(trow[0:1, 0:8],
                                       bass.AP(krx, 0, [[128, 1], [16, 8]])))
            self.sec("vector", vkr0)
            dve_gb = dve.n
            def tgb(t, dve_gb=dve_gb):
                t.wait_ge(dve.h, dve_gb)
                pe.tick(t.transpose(psTB[0:8, 0:1], trow[0:1, 0:8], self.idn[0:1, 0:1]))
            self.sec("tensor", tgb)
            pe_gb = pe.n
            def vgb(v, pe_gb=pe_gb):
                v.wait_ge(pe.h, pe_gb)
                dve.tick(v.tensor_copy(tcol[0:8, 0:1], psTB[0:8, 0:1]))
            self.sec("vector", vgb)
            dve_gc = dve.n
            def tgb2(t, dve_gc=dve_gc):
                t.wait_ge(dve.h, dve_gc)
                pe.tick(t.matmul(psB4[0:1, 0:128], tcol[0:8, 0:1], self.rep8[:],
                                 start=True, stop=True))
            self.sec("tensor", tgb2)
            pe_gb2 = pe.n
            def vgb2(v, pe_gb2=pe_gb2):
                v.wait_ge(pe.h, pe_gb2)
                dve.tick(v.tensor_copy(orow[:], psB4[0:1, 0:128]))
            self.sec("vector", vgb2)
            dve_go = dve.n
            def tgb3(t, dve_go=dve_go):
                t.wait_ge(dve.h, dve_go)
                pe.tick(t.matmul(psB4, self.ones_row[0:1, 0:128], orow[:],
                                 start=True, stop=True))
            self.sec("tensor", tgb3)
            pe_gb3 = pe.n
            def vacc(v, e=e, pe_gb3=pe_gb3):
                v.wait_ge(pe.h, pe_gb3)
                dve.tick(v.tensor_sub(ms["slot"][:], krx[:], psB4))
                v.wait_ge(dve.h, dve.n)
                dve.tick(v.tensor_scalar(ms["t1"][:], ms["slot"][:], float(CLOC),
                                         None, ALU.is_lt))
                dve.tick(v.tensor_scalar_min(ms["slotc"][:], ms["slot"][:], float(CLOC - 1)))
                v.wait_ge(dve.h, dve.n)
                dve.tick(v.tensor_mul(ms["kl"][:], kept[:], ms["t1"][:]))
                dve.tick(v.tensor_scalar_add(ms["stgt"][:], ms["slotc"][:], float(CLOC * e)))
                v.wait_ge(dve.h, dve.n)
                dve.tick(v.tensor_mul(ms["t2"][:], ms["kl"][:], ms["stgt"][:]))
                dve.tick(v.tensor_scalar(ms["t3"][:], ms["kl"][:], -1.0, 1.0,
                                         ALU.mult, ALU.add))
                v.wait_ge(dve.h, dve.n)
                for (omt, dumps, sacc, yacc, wacc, kacc, wct) in (
                        (ohc1, "d1", "s1a", "y1a", "w1a", "k1a", w1c),
                        (ohc2, "d2", "s2a", "y2a", "w2a", "k2a", w2c)):
                    om = omt[:, e * 128:(e + 1) * 128]
                    dve.tick(v.tensor_mul(ms["t4"][:], ms["t3"][:], ms[dumps][:]))
                    v.wait_ge(dve.h, dve.n)
                    dve.tick(v.tensor_add(ms["t4"][:], ms["t4"][:], ms["t2"][:]))
                    v.wait_ge(dve.h, dve.n)
                    dve.tick(v.tensor_mul(ms["t4"][:], ms["t4"][:], om))
                    v.wait_ge(dve.h, dve.n)
                    dve.tick(v.tensor_add(ms[sacc][:], ms[sacc][:], ms["t4"][:]))
                    dve.tick(v.tensor_mul(ms["t4"][:], om, ms["kl"][:]))
                    v.wait_ge(dve.h, dve.n)
                    dve.tick(v.tensor_add(ms[kacc][:], ms[kacc][:], ms["t4"][:]))
                    dve.tick(v.tensor_mul(ms["t1"][:], ms["t4"][:], ms["stgt"][:]))
                    dve.tick(v.tensor_mul(ms["t4"][:], ms["t4"][:], wct[:]))
                    v.wait_ge(dve.h, dve.n)
                    dve.tick(v.tensor_add(ms[yacc][:], ms[yacc][:], ms["t1"][:]))
                    dve.tick(v.tensor_add(ms[wacc][:], ms[wacc][:], ms["t4"][:]))
                    v.wait_ge(dve.h, dve.n)
                    dve.tick(v.tensor_mul(ms["t2"][:], ms["kl"][:], ms["stgt"][:]))
                    dve.tick(v.tensor_scalar(ms["t1"][:], ms["slot"][:], float(CLOC),
                                             None, ALU.is_lt))
                    v.wait_ge(dve.h, dve.n)
            self.sec("vector", vacc)

        def vfin(v):
            v.wait_ge(dve.h, dve.n)
            dve.tick(v.tensor_copy(ms["fbm"][:], fbc[:]))
            v.wait_ge(dve.h, dve.n)
            dve.tick(v.tensor_scalar(ms["t1"][:], ms["fbm"][:], -1.0, 1.0,
                                     ALU.mult, ALU.add))
            v.wait_ge(dve.h, dve.n)
            dve.tick(v.tensor_mul(ms["u1"][:], ms["k1a"][:], ms["t1"][:]))
            dve.tick(v.tensor_mul(ms["u2"][:], ms["k2a"][:], ms["t1"][:]))
            v.wait_ge(dve.h, dve.n)
            for (u, ya, yb) in (("u1", "y1a", "yb1"), ("u2", "y2a", "yb2")):
                dve.tick(v.tensor_mul(ms["t2"][:], ms[u][:], ms[ya][:]))
                dve.tick(v.tensor_scalar(ms["t3"][:], ms[u][:], -1.0, 1.0,
                                         ALU.mult, ALU.add))
                v.wait_ge(dve.h, dve.n)
                dve.tick(v.tensor_mul(ms["t3"][:], ms["t3"][:], ms["dfb"][:]))
                v.wait_ge(dve.h, dve.n)
                dve.tick(v.tensor_add(ms[yb][:], ms["t2"][:], ms["t3"][:]))
            dve.tick(v.tensor_mul(ms["t2"][:], ms["w1a"][:], ms["t1"][:]))
            dve.tick(v.tensor_mul(ms["w2f"][:], ms["w2a"][:], ms["t1"][:]))
            v.wait_ge(dve.h, dve.n)
            dve.tick(v.tensor_add(ms["w1f"][:], ms["t2"][:], ms["fbm"][:]))
        self.sec("vector", vfin)

        for si, (srck, dstk) in enumerate((("s1a", "g1"), ("s2a", "g2"),
                                           ("yb1", "y1"), ("yb2", "y2"))):
            def vcv(v, srck=srck):
                v.wait_ge(dve.h, dve.n)
                v.wait_ge(self.gd.h, gd.n)
                dve.tick(v.tensor_copy(cvt[:], ms[srck][:]))
            self.sec("vector", vcv)
            dve_cv = dve.n
            def gbn(g, si=si, dstk=dstk, dve_cv=dve_cv, ld_s40=ld_s40):
                g.wait_ge(dve.h, dve_cv)
                g.wait_ge(ld.h, ld_s40)
                gd.tick(g.dma_start(out=bass.AP(self.dr["idxb"].tensor, si * 16384,
                                                [[128, NP], [1, 128]]),
                                    in_=cvt[:]), 16)
                g.wait_ge(gd.h, gd.n)
                RR = g.value_load(csl[0:1, 0:1], min_val=0, max_val=7)
                gd.tick(g.dma_start(
                    out=ind_idx[dstk][:],
                    in_=_mk_runtime_ap(self.dr["idxb"], si * 16384, RR * 16,
                                       [[128, NP], [1, 16]]),
                    allow_slow_non_contiguous=True), 16)
            self.sec("gpsimd", gbn)
        gd_idx = gd.n
        def gdump(g):
            g.wait_ge(gd.h, gd_idx)
            for si, k in enumerate(("g1", "g2", "y1", "y2")):
                gd.tick(g.dma_start(out=bass.AP(self.dr["idxwd"].tensor, si * 2048,
                                                [[16, NP], [1, 16]]),
                                    in_=ind_idx[k][:]), 16)
        self.sec("gpsimd", gdump)

        for si, (srck, dstk) in enumerate((("w1f", "w1"), ("w2f", "w2"))):
            def vcw(v, srck=srck):
                v.wait_ge(dve.h, dve.n)
                v.wait_ge(self.gd.h, gd.n)
                dve.tick(v.tensor_copy(rs1[:], ms[srck][:]))
            self.sec("vector", vcw)
            dve_cw = dve.n
            def gbw(g, si=si, dstk=dstk, dve_cw=dve_cw):
                g.wait_ge(dve.h, dve_cw)
                gd.tick(g.dma_start(out=bass.AP(self.dr["wfb"].tensor, si * 16384,
                                                [[128, NP], [1, 128]]),
                                    in_=rs1[:]), 16)
                g.wait_ge(gd.h, gd.n)
                RR = g.value_load(csl[0:1, 0:1], min_val=0, max_val=7)
                gd.tick(g.dma_start(
                    out=wf16[dstk][:],
                    in_=_mk_runtime_ap(self.dr["wfb"], si * 16384, RR * 16,
                                       [[128, NP], [1, 16]]),
                    allow_slow_non_contiguous=True), 16)
            self.sec("gpsimd", gbw)
        gd_wf = gd.n
        if self.p4mode == "route":
            stkF.close()
            stk4a.close()
            stkR.close()
            return
        stkF.close()
        stk4a.close()

        stkS = ExitStack()
        h2til = [self.sb(f"h2til{i}", [NP, 1024], stk=stkS) for i in range(2)]
        gd_sc_base = gd.n
        ld_h2t_marks = {}
        def ssc_all(sync, gd_wf=gd_wf):
            sync.wait_ge(gd.h, gd_wf)
            for tt in range(16):
                if tt >= 2:
                    sync.wait_ge(gd.h, gd_sc_base + 32 * (tt - 1))
                ld.tick(sync.dma_start(out=h2til[tt % 2][:],
                                       in_=self.dr["h2tm"][tt * NP:(tt + 1) * NP, :]), 16)
                ld_h2t_marks[tt] = ld.n
        self.sec("sync", ssc_all)
        def gsc_all(g, gd_idx=gd_idx, gd_zero=gd_zero):
            g.wait_ge(gd.h, max(gd_idx, gd_zero))
            for tt in range(16):
                g.wait_ge(ld.h, ld_h2t_marks[tt])
                gd.tick(g.indirect_dma_start(
                    self.dr["gbuf"][:, :],
                    bass.IndirectOffsetOnAxis(ind_idx["g1"][:, tt:tt + 1], 0),
                    h2til[tt % 2][:], None), 16)
                gd.tick(g.indirect_dma_start(
                    self.dr["gbuf"][:, :],
                    bass.IndirectOffsetOnAxis(ind_idx["g2"][:, tt:tt + 1], 0),
                    h2til[tt % 2][:], None), 16)
        self.sec("gpsimd", gsc_all)
        gd_scat = gd.n
        stkS.close()
        if self.p4mode == "scatter":
            stkR.close()
            return

        stkE = ExitStack()
        gtile = [self.sb(f"gtile{i}", [NP, 1024], stk=stkE) for i in range(2)]
        gf = self.sb("gf", [NP, 8 * CLOC], stk=stkE)
        w1blk = self.sb("w1blk", [NP, 8 * 512], stk=stkE)
        w2te = [self.sb(f"w2te{i}", [NP, 512], stk=stkE) for i in range(2)]
        a1g = [self.sb(f"a1g{i}", [NP, CLOC], stk=stkE) for i in range(32)]
        b1c = self.sb("b1c", [NP, 32], stk=stkE)
        b2b = self.sb("b2b", [NP, D], stk=stkE)
        brow2 = self.sb("brow2", [1, D], stk=stkE)
        ystgE = [self.sb(f"ystgE{i}", [NP, 512], stk=stkE) for i in range(2)]
        pse6 = [self.ps(f"pse{i}", [NP, 512], stk=stkE) for i in range(6)]
        psg = pse6[0:2]
        psm1 = pse6[0:4]
        psE = pse6[0:6]

        for e in range(8):
            def se0(sync, e=e):
                sync.wait_ge(act.h, act.n)
                sync.wait_ge(dve.h, dve.n)
                ld.tick(sync.dma_start(out=b1c[:], in_=bass.AP(
                    self.inp["b1"].tensor, e * DI, [[1, NP], [NP, 32]]),
                    allow_slow_non_contiguous=True), 16)
                ld.tick(sync.dma_start(out=brow2[0:1, :], in_=bass.AP(
                    self.inp["b2"].tensor, e * D, [[D, 1], [1, D]])), 16)
            self.sec("sync", se0)
            ld_e0 = ld.n
            def te0(t, ld_e0=ld_e0):
                t.wait_ge(ld.h, ld_e0)
                t.wait_ge(dve.h, dve.n)
                for sgb in range(2):
                    pe.tick(t.matmul(pse6[4 + sgb][:], self.ones_row[0:1, 0:128],
                                     brow2[0:1, sgb * 512:(sgb + 1) * 512],
                                     start=True, stop=True))
            self.sec("tensor", te0)
            pe_b2 = pe.n
            def ve0(v, pe_b2=pe_b2):
                v.wait_ge(pe.h, pe_b2)
                for sgb in range(2):
                    dve.tick(v.tensor_copy(b2b[:, sgb * 512:(sgb + 1) * 512],
                                           pse6[4 + sgb][:]))
            self.sec("vector", ve0)
            pool_b2 = dve.n

            for s6 in range(6):
                def sgl(sync, e=e, s6=s6, gd_scat=gd_scat):
                    if e == 0 and s6 == 0:
                        sync.wait_ge(gd.h, gd_scat)
                        sync.wait_ge(dve.h, dve.n)
                    if s6 >= 2 or e > 0:
                        sync.wait_ge(pe.h, pe.n)
                    r0 = e * CLOC + s6 * NP
                    ld.tick(sync.dma_start(out=gtile[s6 % 2][:],
                                           in_=self.dr["gbuf"][r0:r0 + NP, :]), 16)
                self.sec("sync", sgl)
                ld_g = ld.n
                for half in range(2):
                    def tgt(t, s6=s6, half=half, ld_g=ld_g):
                        t.wait_ge(ld.h, ld_g)
                        t.wait_ge(act.h, act.n)
                        if s6 == 0 and half == 0:
                            t.wait_ge(dve.h, dve.n)
                        for q in range(4):
                            dc = half * 4 + q
                            pe.tick(t.transpose(psg[half][:, q * 128:(q + 1) * 128],
                                                gtile[s6 % 2][:, dc * 128:(dc + 1) * 128],
                                                self.idn[:]))
                    self.sec("tensor", tgt)
                    pe_g = pe.n
                    def agt(s, s6=s6, half=half, pe_g=pe_g):
                        s.wait_ge(pe.h, pe_g)
                        act.tick(s.activation(
                            bass.AP(gf, (half * 4) * CLOC + s6 * 128,
                                    [[8 * CLOC, NP], [CLOC, 4], [1, 128]]),
                            psg[half][:], AF.Copy))
                    self.sec("scalar", agt)
            act_g = act.n

            pe_e1 = pe.n
            act_e1 = act.n
            for ibk in range(8):
                def sw1(sync, e=e, ibk=ibk, pe_e1=pe_e1):
                    if ibk >= 1:
                        sync.wait_ge(pe.h, pe_e1 + 64 * ibk)
                    for dc in range(8):
                        ld.tick(sync.dma_start(
                            out=w1blk[:, dc * 512:(dc + 1) * 512],
                            in_=bass.AP(self.inp["w1"].tensor,
                                        e * D * DI + dc * NP * DI + ibk * 512,
                                        [[DI, NP], [1, 512]])), 16)
                self.sec("sync", sw1)
                ld_w1 = ld.n
                def tme1(t, ibk=ibk, ld_w1=ld_w1, act_g=act_g, act_e1=act_e1):
                    t.wait_ge(ld.h, ld_w1)
                    if ibk == 0:
                        t.wait_ge(act.h, act_g)
                    for ic2 in range(4):
                        ic = ibk * 4 + ic2
                        if ic >= 2:
                            t.wait_ge(act.h, act_e1 + 2 * (ic - 1))
                        for tb2 in range(2):
                            t0w = 512 * tb2
                            tw = 512 if tb2 == 0 else CLOC - 512
                            for dc in range(8):
                                pe.tick(t.matmul(
                                    psm1[2 * (ic % 2) + tb2][0:NP, 0:tw],
                                    w1blk[:, dc * 512 + ic2 * 128: dc * 512 + (ic2 + 1) * 128],
                                    bass.AP(gf, dc * CLOC + t0w, [[8 * CLOC, NP], [1, tw]]),
                                    start=(dc == 0), stop=(dc == 7)))
                self.sec("tensor", tme1)
                def age1(s, e=e, ibk=ibk, pe_e1=pe_e1, ld_e0=ld_e0):
                    if ibk == 0:
                        s.wait_ge(ld.h, ld_e0)
                    for ic2 in range(4):
                        ic = ibk * 4 + ic2
                        s.wait_ge(pe.h, pe_e1 + 64 * ibk + 16 * (ic2 + 1))
                        act.tick(s.activation(a1g[ic][:, 0:512],
                                              psm1[2 * (ic % 2)][:], AF.Gelu,
                                              bias=b1c[:, ic:ic + 1]))
                        act.tick(s.activation(a1g[ic][:, 512:CLOC],
                                              psm1[2 * (ic % 2) + 1][0:NP, 0:CLOC - 512],
                                              AF.Gelu, bias=b1c[:, ic:ic + 1]))
                self.sec("scalar", age1)
            act_gel_e = act.n

            for ob in range(2):
                pe_ob2 = pe.n
                dve_evp = dve.n
                for ic in range(32):
                    def sw2(sync, e=e, ic=ic, ob=ob, pe_ob2=pe_ob2):
                        if ic >= 2:
                            sync.wait_ge(pe.h, pe_ob2 + 6 * (ic - 1))
                        ld.tick(sync.dma_start(
                            out=w2te[ic % 2][:],
                            in_=bass.AP(self.inp["w2"].tensor,
                                        e * DI * D + ic * NP * D + ob * 512,
                                        [[D, NP], [1, 512]])), 16)
                    self.sec("sync", sw2)
                    ld_w2 = ld.n
                    def tme2(t, ic=ic, ld_w2=ld_w2, act_gel_e=act_gel_e, dve_evp=dve_evp):
                        t.wait_ge(ld.h, ld_w2)
                        if ic == 0:
                            t.wait_ge(act.h, act_gel_e)
                            t.wait_ge(dve.h, dve_evp)
                        for tt6 in range(6):
                            pe.tick(t.matmul(psE[tt6][:],
                                             a1g[ic][:, tt6 * 128:(tt6 + 1) * 128],
                                             w2te[ic % 2][:],
                                             start=(ic == 0), stop=(ic == 31)))
                    self.sec("tensor", tme2)
                pe_me2 = pe.n
                for tt6 in range(6):
                    def vevE(v, tt6=tt6, ob=ob, pe_me2=pe_me2, pool_b2=pool_b2):
                        v.wait_ge(pe.h, pe_me2)
                        if tt6 == 0:
                            v.wait_ge(dve.h, pool_b2)
                        if tt6 >= 2:
                            v.wait_ge(ld.h, ld.n)
                        dve.tick(v.tensor_add(ystgE[tt6 % 2][:], psE[tt6][:],
                                              b2b[:, ob * 512:(ob + 1) * 512]))
                    self.sec("vector", vevE)
                    dve_ev = dve.n
                    def sstE(sync, e=e, ob=ob, tt6=tt6, dve_ev=dve_ev):
                        sync.wait_ge(dve.h, dve_ev)
                        r0 = e * CLOC + tt6 * 128
                        ld.tick(sync.dma_start(
                            out=self.dr["ybuf"][r0:r0 + NP, ob * 512:(ob + 1) * 512],
                            in_=ystgE[tt6 % 2][:]), 16)
                    self.sec("sync", sstE)
        stkE.close()
        if self.p4mode == "expert":
            stkR.close()
            return

        stkG = ExitStack()
        g1t = self.sb("g1t", [NP, 1024], stk=stkG)
        g2t = self.sb("g2t", [NP, 1024], stk=stkG)
        x2til = self.sb("x2til", [NP, 1024], stk=stkG)
        ytil = self.sb("ytil", [NP, 1024], stk=stkG)
        ld_allst = ld.n
        gd_ga_base = gd.n
        dve_fin_base = dve.n
        ld_x2_marks = {}
        ld_st_marks = {}
        def sxy_all(sync, ld_allst=ld_allst):
            sync.wait_ge(ld.h, ld_allst)
            for tt in range(16):
                if tt >= 1:
                    sync.wait_ge(dve.h, dve_fin_base + 5 * tt)
                    ld.tick(sync.dma_start(
                        out=self.out_y[(tt - 1) * NP:tt * NP, :], in_=ytil[:]), 16)
                    ld_st_marks[tt - 1] = ld.n
                ld.tick(sync.dma_start(out=x2til[:],
                                       in_=self.dr["x2d"][tt * NP:(tt + 1) * NP, :]), 16)
                ld_x2_marks[tt] = ld.n
            sync.wait_ge(dve.h, dve_fin_base + 80)
            ld.tick(sync.dma_start(out=self.out_y[15 * NP:16 * NP, :], in_=ytil[:]), 16)
            ld_st_marks[15] = ld.n
        self.sec("sync", sxy_all)
        def gga_all(g, ld_allst=ld_allst, gd_wf=gd_wf):
            g.wait_ge(ld.h, ld_allst)
            g.wait_ge(gd.h, gd_wf)
            for tt in range(16):
                if tt >= 1:
                    g.wait_ge(dve.h, dve_fin_base + 5 * tt)
                gd.tick(g.indirect_dma_start(
                    g1t[:], None, self.dr["ybuf"][:, :],
                    bass.IndirectOffsetOnAxis(ind_idx["y1"][:, tt:tt + 1], 0)), 16)
                gd.tick(g.indirect_dma_start(
                    g2t[:], None, self.dr["ybuf"][:, :],
                    bass.IndirectOffsetOnAxis(ind_idx["y2"][:, tt:tt + 1], 0)), 16)
        self.sec("gpsimd", gga_all)
        def vfin_all(v):
            for tt in range(16):
                v.wait_ge(gd.h, gd_ga_base + 32 * (tt + 1))
                v.wait_ge(ld.h, ld_x2_marks[tt])
                if tt >= 1:
                    v.wait_ge(ld.h, ld_st_marks[tt - 1])
                dve.tick(v.tensor_scalar(g1t[:], g1t[:], wf16["w1"][:, tt:tt + 1],
                                         None, ALU.mult))
                dve.tick(v.tensor_scalar(g2t[:], g2t[:], wf16["w2"][:, tt:tt + 1],
                                         None, ALU.mult))
                v.wait_ge(dve.h, dve.n)
                dve.tick(v.tensor_add(g1t[:], g1t[:], g2t[:]))
                v.wait_ge(dve.h, dve.n)
                dve.tick(v.tensor_mul(g1t[:], g1t[:], self.mod["gb"][:]))
                v.wait_ge(dve.h, dve.n)
                dve.tick(v.tensor_add(ytil[:], g1t[:], x2til[:]))
        self.sec("vector", vfin_all)
        stkG.close()
        stkR.close()

    def emit_taps(self):
        nc, ld = self.nc, self.ld
        outs = []
        for tap in self.taps:
            if len(tap) == 4:
                (oname, dname, shape, dt_) = tap
            else:
                (oname, dname, shape) = tap
                dt_ = F32
            outs.append((nc.dram_tensor(oname, shape, dt_, kind="ExternalOutput").ap(),
                         dname, shape))
        def st(sync):
            for s in (self.pe, self.act, self.dve, self.pool, self.gd, self.cc, self.ld):
                if s.n:
                    sync.wait_ge(s.h, s.n)
            for (oap, dname, shape) in outs:
                rows = shape[0]
                src_ap = self.dr[dname]
                r0 = 0
                while r0 < rows:
                    rr = min(512, rows - r0)
                    if len(src_ap.shape) == 1:
                        ncols = shape[1]
                        sl = src_ap[r0 * ncols:(r0 + rr) * ncols].rearrange(
                            "(a b) -> a b", b=ncols)
                    else:
                        sl = src_ap[r0:r0 + rr, :]
                    ld.tick(sync.dma_start(out=oap[r0:r0 + rr, :], in_=sl), 16)
                    r0 += rr
            sync.wait_ge(ld.h, ld.n)
        self.sec("sync", st)


def build(taps=(), p4mode="full"):
    kb = KB(taps=taps, p4mode=p4mode)
    return kb.build()


def make_in_maps(inputs):
    x = np.ascontiguousarray(np.asarray(inputs["L0_attn_x"], dtype=np.float32))
    Wq = np.asarray(inputs["L0_attn_Wq"], np.float32)
    Wk = np.asarray(inputs["L0_attn_Wk"], np.float32)
    Wv = np.asarray(inputs["L0_attn_Wv"], np.float32)
    bq = np.asarray(inputs["L0_attn_bq"], np.float32)
    bk = np.asarray(inputs["L0_attn_bk"], np.float32)
    bv = np.asarray(inputs["L0_attn_bv"], np.float32)
    Wo = np.asarray(inputs["L0_attn_Wo"], np.float32)
    bo = np.asarray(inputs["L0_attn_bo"], np.float32)
    rb = np.asarray(inputs["L0_attn_rel_bias"], np.float32)
    maps = []
    for c in range(8):
        b, hh = c // 2, c % 2
        hs = slice(512 * hh, 512 * (hh + 1))
        m = {
            "x": np.ascontiguousarray(x[b]),
            "xo": np.ascontiguousarray(x[b, TOK * hh: TOK * (hh + 1)]),
            "wq": np.ascontiguousarray(Wq[hs]), "bq": np.ascontiguousarray(bq[hs]),
            "wk": np.ascontiguousarray(Wk[hs]), "bk": np.ascontiguousarray(bk[hs]),
            "wv": np.ascontiguousarray(Wv[hs]), "bv": np.ascontiguousarray(bv[hs]),
            "wo": np.ascontiguousarray(Wo[:, hs]), "bo": bo,
            "rb": np.ascontiguousarray(rb[:, 8 * hh: 8 * (hh + 1)]),
            "wr": np.asarray(inputs["L0_ffn_Wr"], np.float32),
            "w1": np.asarray(inputs["L0_ffn_W1"], np.float32),
            "b1": np.asarray(inputs["L0_ffn_b1"], np.float32),
            "w2": np.asarray(inputs["L0_ffn_W2"], np.float32),
            "b2": np.asarray(inputs["L0_ffn_b2"], np.float32),
            "fw1": np.asarray(inputs["L0_ffn_fw1"], np.float32),
            "fb1": np.asarray(inputs["L0_ffn_fb1"], np.float32),
            "fw2": np.asarray(inputs["L0_ffn_fw2"], np.float32),
            "fb2": np.asarray(inputs["L0_ffn_fb2"], np.float32),
            "cmod": np.ascontiguousarray(np.asarray(inputs["L0_c"], np.float32)[b:b + 1]),
            "wmod": np.asarray(inputs["L0_wmod"], np.float32),
            "cslot": np.array([[c]], dtype=np.int32),
        }
        maps.append(m)
    return maps


def kernel(**inputs):
    nc = build()
    in_maps = make_in_maps(inputs)
    res = run_bass_kernel_spmd(nc, in_maps, core_ids=list(range(8)))
    y = np.empty((4, T, D), np.float32)
    for c in range(8):
        b, hh = c // 2, c % 2
        y[b, TOK * hh: TOK * (hh + 1)] = res.results[c]["y"]
    return y
